# version 05
# baseline (speedup 1.0000x reference)
import contextlib
import numpy as np
import concourse.bass as bass
import concourse.mybir as mybir
from concourse.bass_utils import run_bass_kernel_spmd
from concourse.bass_core_rust import IndirectOffsetOnAxis

F32 = mybir.dt.float32
F32R = mybir.dt.float32r
BF16 = mybir.dt.bfloat16
I32 = mybir.dt.int32
AF = mybir.ActivationFunctionType
ALU = mybir.AluOpType
AXX = mybir.AxisListType.X

B, S, D = 16, 2048, 1024
H, DK = 16, 64
NCORE = 8
BC = B // NCORE
T = BC * S
NT = T // 128
NTB = S // 128

NN, NL = 255, 256
GRP, NG, CAP = 7, 37, 320
SLOTS = NG * CAP
TRASH = SLOTS
EXT = 1152
CHG = 4
NCH = (NG + CHG - 1) // CHG
NEG = -30000.0


def _dfs_order():
    out = []
    def rec(i):
        if i >= NN:
            return
        out.append(i)
        rec(2 * i + 1)
        rec(2 * i + 2)
    rec(0)
    return np.array(out)

DFS = _dfs_order()


def _prep_host(ins):
    a = {}
    def nwT(name):
        w = np.asarray(ins[f"L0_ffn_{name}_nw"], np.float32)[DFS]
        return np.ascontiguousarray(w.T.reshape(8, 128, NN).transpose(1, 0, 2))
    pad = np.zeros((128, 8, 1), np.float32)
    a["nwT_qkv"] = np.ascontiguousarray(np.concatenate(
        [nwT("q"), pad, nwT("k"), pad, nwT("v"), pad], axis=2))
    a["nwT_f"] = np.ascontiguousarray(np.concatenate([nwT("f"), pad], axis=2))

    def nbrow(name):
        v = np.asarray(ins[f"L0_ffn_{name}_nb"], np.float32)[DFS]
        return np.concatenate([v, np.zeros(1, np.float32)])
    a["nb_qkv"] = np.ascontiguousarray(
        np.concatenate([nbrow("q"), nbrow("k"), nbrow("v")])[None, :])
    a["nb_f"] = np.ascontiguousarray(nbrow("f")[None, :])

    leaf_of = np.zeros((NG, GRP), np.int64)
    for g in range(NG):
        for b_ in range(GRP):
            lf = b_ * NG + g
            leaf_of[g, b_] = lf if lf < NL else 0
    for name in ("q", "k", "v", "f"):
        w1 = np.asarray(ins[f"L0_ffn_{name}_w1"], np.float32)
        b1 = np.asarray(ins[f"L0_ffn_{name}_b1"], np.float32)
        w2 = np.asarray(ins[f"L0_ffn_{name}_w2"], np.float32)
        b2 = np.asarray(ins[f"L0_ffn_{name}_b2"], np.float32)
        W1 = w1[leaf_of].reshape(NG, GRP, 8, 128, 16)
        a[f"W1_{name}"] = np.ascontiguousarray(
            W1.transpose(3, 0, 2, 1, 4).reshape(128, NG, 8, GRP * 16))
        a[f"b1_{name}"] = np.ascontiguousarray(b1[leaf_of].reshape(1, NG * GRP * 16))
        W2 = np.zeros((128, NG, D), np.float32)
        W2[:112] = w2[leaf_of].reshape(NG, GRP * 16, D).transpose(1, 0, 2)
        W2[112:119] = b2[leaf_of].transpose(1, 0, 2)
        a[f"W2_{name}"] = np.ascontiguousarray(W2)

    a["WdqT"] = np.ascontiguousarray(np.asarray(ins["L0_attn_W_dq"], np.float32)
                                     .reshape(8, 128, 2, 128).transpose(1, 0, 2, 3))
    a["WdkvT"] = np.ascontiguousarray(np.asarray(ins["L0_attn_W_dkv"], np.float32)
                                      .reshape(8, 128, 2, 128).transpose(1, 0, 2, 3))
    a["WuqT"] = np.ascontiguousarray(np.asarray(ins["L0_attn_W_uq"], np.float32)
                                     .reshape(2, 128, 8, 128).transpose(1, 0, 2, 3))
    a["WukT"] = np.ascontiguousarray(np.asarray(ins["L0_attn_W_uk"], np.float32)
                                     .reshape(2, 128, 8, 128).transpose(1, 0, 2, 3))
    a["Wuv"] = np.ascontiguousarray(np.asarray(ins["L0_attn_W_uv"], np.float32)
                                    .reshape(2, 128, D).transpose(1, 0, 2))
    a["Wo"] = np.ascontiguousarray(np.asarray(ins["L0_attn_W_o"], np.float32)
                                   .reshape(8, 128, D).transpose(1, 0, 2))
    return a


class Ctr:
    def __init__(self, h):
        self.h = h
        self.n = 0
    def inc(self, k=1):
        self.n += k
        return self.n


class _Cut(Exception):
    pass


SEM_TRACE = {}
import os as _os0
_DBG_DEST = _os0.environ.get("K_DBG_DEST") == "1"
_DBG_AV = _os0.environ.get("K_DBG_AV") == "1"
_DBG_LN = _os0.environ.get("K_DBG_LN") == "1"


def build_program(upto=None):
    nc = bass.Bass("TRN2", target_bir_lowering=False, detect_race_conditions=False)

    x_in = nc.dram_tensor("x", [T, D], F32, kind="ExternalInput").ap()
    out_d = nc.dram_tensor("out", [T, D], F32, kind="ExternalOutput").ap()
    hi = {}
    for nm, shape in [("nwT_qkv", [128, 8, 768]), ("nwT_f", [128, 8, 256]),
                      ("nb_qkv", [1, 768]), ("nb_f", [1, 256]),
                      ("WdqT", [128, 8, 2, 128]), ("WdkvT", [128, 8, 2, 128]),
                      ("WuqT", [128, 2, 8, 128]), ("WukT", [128, 2, 8, 128]),
                      ("Wuv", [128, 2, D]), ("Wo", [128, 8, D])]:
        hi[nm] = nc.dram_tensor(nm, shape, F32, kind="ExternalInput").ap()
    for n in ("q", "k", "v", "f"):
        hi[f"W1_{n}"] = nc.dram_tensor(f"W1_{n}", [128, NG, 8, GRP * 16], F32,
                                       kind="ExternalInput").ap()
        hi[f"b1_{n}"] = nc.dram_tensor(f"b1_{n}", [1, NG * GRP * 16], F32,
                                       kind="ExternalInput").ap()
        hi[f"W2_{n}"] = nc.dram_tensor(f"W2_{n}", [128, NG, D], F32,
                                       kind="ExternalInput").ap()

    d_x16 = nc.dram_tensor("d_x16", [T, D], BF16, kind="Internal").ap()
    d_x1n = nc.dram_tensor("d_x1n", [T, D], F32, kind="Internal").ap()
    d_xn32 = nc.dram_tensor("d_xn32", [T, D], F32, kind="Internal").ap()
    d_xn16 = nc.dram_tensor("d_xn16", [T, D], BF16, kind="Internal").ap()
    d_y32 = nc.dram_tensor("d_y32", [T, D], F32, kind="Internal").ap()
    d_y16 = nc.dram_tensor("d_y16", [T, D], BF16, kind="Internal").ap()
    d_qcan = nc.dram_tensor("d_qcan", [T, D], BF16, kind="Internal").ap()
    d_kcan = nc.dram_tensor("d_kcan", [T, D], BF16, kind="Internal").ap()
    d_vcan = nc.dram_tensor("d_vcan", [T, D], BF16, kind="Internal").ap()
    d_sortx = nc.dram_tensor("d_sortx", [12032, EXT], BF16, kind="Internal").ap()
    d_sorto = nc.dram_tensor("d_sorto", [SLOTS + 128, D], BF16, kind="Internal").ap()

    root = contextlib.ExitStack()
    with root:
        LD = Ctr(root.enter_context(nc.semaphore("LD")))
        LS = Ctr(root.enter_context(nc.semaphore("LS")))
        LG = Ctr(root.enter_context(nc.semaphore("LG")))
        TS = Ctr(root.enter_context(nc.semaphore("TSE")))
        VS = Ctr(root.enter_context(nc.semaphore("VSE")))
        SS = Ctr(root.enter_context(nc.semaphore("SSE")))
        GS = Ctr(root.enter_context(nc.semaphore("GSE")))
        OUT = Ctr(root.enter_context(nc.semaphore("OUTS")))

        def sbuf(st, name, shape, dt=BF16):
            return st.enter_context(nc.sbuf_tensor(name, list(shape), dt))

        def psum(st, name, shape):
            return st.enter_context(nc.psum_tensor(name, list(shape), F32))

        iden = sbuf(root, "iden", (128, 128), F32)
        iden16 = sbuf(root, "iden16", (128, 128), BF16)
        maskdiag = sbuf(root, "maskdiag", (128, 128), BF16)
        tristrict = sbuf(root, "tristrict", (128, 128), BF16)
        upperincl = sbuf(root, "upperincl", (128, 128), BF16)
        allones = sbuf(root, "allones", (128, 128), BF16)
        iota37 = sbuf(root, "iota37", (128, NG), F32)
        bandidx = sbuf(root, "bandidx", (128, 1), F32)
        ones_r = sbuf(root, "ones_r", (1, 128), F32)
        ones_320 = sbuf(root, "ones_320", (1, CAP), BF16)
        ones_col = sbuf(root, "ones_col", (128, 1), BF16)
        ist = sbuf(root, "ist", (128, 8), I32)
        dest_i = {n: sbuf(root, f"dest_{n}", (128, NT), I32) for n in "qkvf"}
        leafb = {n: sbuf(root, f"leaf_{n}", (128, NT), F32) for n in "qkvf"}
        bnd = {n: sbuf(root, f"bnd_{n}", (128, NT), F32) for n in "qkvf"}
        pw = sbuf(root, "pw", (128, 24, 8), F32)
        xfa = sbuf(root, "xfa", (128, D), F32)
        xfb = sbuf(root, "xfb", (128, D), F32)
        tfa = sbuf(root, "tfa", (128, D), F32)
        tfb = sbuf(root, "tfb", (128, D), F32)
        hfa = sbuf(root, "hfa", (128, D), BF16)
        hfb = sbuf(root, "hfb", (128, D), BF16)
        bn6 = sbuf(root, "bn6", (128, 12), F32)
        bn2 = sbuf(root, "bn2", (128, 2), F32)
        st1 = sbuf(root, "st1", (128, 1), F32)
        st2 = sbuf(root, "st2", (128, 1), F32)
        st3 = sbuf(root, "st3", (128, 1), F32)
        c_eps = sbuf(root, "c_eps", (128, 1), F32)
        pidx = sbuf(root, "pidx", (128, 1), F32)
        cmpv = sbuf(root, "cmpv", (128, 1), F32)
        fence = sbuf(root, "fence", (128, 1), F32)

        block = root.enter_context(nc.Block())
        ev = {}

        def barrier(tag):
            if tag.startswith("dbg_") and tag != upto:
                return
            _do_barrier(tag)
            if upto is not None and tag == upto:
                raise _Cut()

        def _do_barrier(tag):
            SEM_TRACE[tag] = dict(TS=TS.n, VS=VS.n, SS=SS.n, GS=GS.n,
                                  LD=LD.n, LS=LS.n, LG=LG.n, OUT=OUT.n)
            vals = [(TS, TS.n), (VS, VS.n), (SS, SS.n), (GS, GS.n),
                    (LD, LD.n), (LS, LS.n), (LG, LG.n)]
            def waits(e):
                for c, n in vals:
                    if n:
                        e.wait_ge(c.h, n)
            @block.gpsimd
            def _(g):
                waits(g)
            @block.tensor
            def _(tn):
                waits(tn)
            @block.vector
            def _(v):
                waits(v)
            @block.scalar
            def _(s):
                waits(s)
            @block.sync
            def _(sy):
                waits(sy)

        @block.gpsimd
        def _(g):
            g.iota(ist[:, 0:1], pattern=[[0, 1]], base=0, channel_multiplier=1)
            g.iota(ist[:, 1:1 + 1], pattern=[[0, 1]], base=0, channel_multiplier=0).then_inc(GS.h, 1)
        GS.inc()
        i37 = sbuf(root, "i37", (128, NG), I32)
        @block.gpsimd
        def _(g):
            g.iota(i37[:], pattern=[[1, NG]], base=0, channel_multiplier=0).then_inc(GS.h, 1)
        GS.inc()

        def vw(v, mm):
            mm.then_inc(VS.h, 1)
            v.wait_ge(VS.h, VS.inc())

        @block.vector
        def _(v):
            v.wait_ge(GS.h, GS.n)
            vw(v, v.tensor_copy(st1[:], ist[:, 0:1]))
            v.tensor_copy(pidx[:], ist[:, 0:1])
            v.tensor_copy(iota37[:], i37[:])
            vw(v, v.tensor_scalar_mul(st2[:], st1[:], 1.0 / 16.0))
            vw(v, v.tensor_copy(ist[:, 2:3], st2[:]))
            vw(v, v.tensor_copy(bandidx[:], ist[:, 2:3]))
            vw(v, v.tensor_tensor(st2[:], st2[:], bandidx[:], ALU.is_lt))
            vw(v, v.tensor_sub(bandidx[:], bandidx[:], st2[:]))
            vw(v, v.tensor_scalar_add(st3[:], pidx[:], -112.0))
            vw(v, v.tensor_single_scalar(st2[:], bandidx[:], 7.0, ALU.is_equal))
            vw(v, v.tensor_sub(st3[:], st3[:], bandidx[:]))
            vw(v, v.scalar_tensor_tensor(cmpv[:], st3[:], st2[:], bandidx[:],
                                         ALU.mult, ALU.add))
            for _k in range(8):
                v.memset(pw[:, :, _k:_k + 1], float(2 ** (7 - _k)))
            v.memset(ones_r[:], 1.0)
            for _n in "qkvf":
                v.memset(dest_i[_n][:], TRASH)
            v.memset(ones_320[:], 1.0)
            v.memset(ones_col[:], 1.0)
            v.memset(c_eps[:], 1e-5)
            v.memset(tfa[:, 0:128], 1.0)
            v.memset(hfa[:, 0:128], NEG)
            v.memset(hfb[:, 0:128], 1.0).then_inc(VS.h, 1)
        VS.inc()

        @block.gpsimd
        def _(g):
            g.wait_ge(VS.h, VS.n)
            g.affine_select(tfb[:, 0:128], tfa[:, 0:128], pattern=[[-1, 128]],
                            compare_op=ALU.is_ge, fill=0.0, base=0, channel_multiplier=1)
            g.affine_select(iden[:], tfb[:, 0:128], pattern=[[1, 128]],
                            compare_op=ALU.is_ge, fill=0.0, base=0, channel_multiplier=-1)
            g.affine_select(maskdiag[:], hfa[:, 0:128], pattern=[[-1, 128]],
                            compare_op=ALU.is_ge, fill=0.0, base=-1, channel_multiplier=1)
            g.affine_select(tristrict[:], hfb[:, 0:128], pattern=[[1, 128]],
                            compare_op=ALU.is_ge, fill=0.0, base=-1, channel_multiplier=-1)
            g.affine_select(upperincl[:], hfb[:, 0:128], pattern=[[-1, 128]],
                            compare_op=ALU.is_ge, fill=0.0, base=0, channel_multiplier=1)
            g.memset(allones[:], 1.0)
            g.tensor_copy(iden16[:], iden[:]).then_inc(GS.h, 1)
        GS.inc()
        _do_barrier("p0")

        _SKIP_ZF = False
        @block.vector
        def _(v):
            v.memset(hfa[:], 0.0)
            v.memset(hfb[:, 0:128], 0.0).then_inc(VS.h, 1)
        ev_zf = VS.inc()
        @block.sync
        def _(sy):
            sy.wait_ge(VS.h, ev_zf)
            for i in range(0 if _SKIP_ZF else 94):
                sy.dma_start(out=d_sortx[i * 128:(i + 1) * 128, 0:1024],
                             in_=hfa[:]).then_inc(LD.h, 16)
                sy.dma_start(out=d_sortx[i * 128:(i + 1) * 128, 1024:1152],
                             in_=hfb[:, 0:128]).then_inc(LD.h, 16)
                LD.inc(32)
            if not _SKIP_ZF:
                sy.dma_start(out=d_sorto[SLOTS:SLOTS + 128, :],
                             in_=hfa[:].bitcast(BF16) if False else hfa[:]).then_inc(LD.h, 16)
                LD.inc(16)
        try:
            barrier("zf")
        except _Cut:
            pass
        _zf_cut = (upto == "zf")


        def mla_batch(b):
            st = contextlib.ExitStack()
            _cut = [False]
            with st:
              try:
                cqT = sbuf(st, f"cqT{b}", (128, 2, S))
                ckvT = sbuf(st, f"ckvT{b}", (128, 2, S))
                QT = sbuf(st, f"QT{b}", (128, 8, S))
                KT = sbuf(st, f"KT{b}", (128, 8, S))
                Vp = sbuf(st, f"Vp{b}", (128, NTB, H * 65))
                p_a = psum(st, f"p_a{b}", (128, 2048))
                stA = st.enter_context(contextlib.ExitStack())
                xT = sbuf(stA, f"xT{b}", (128, 8, S))
                wdq = sbuf(stA, f"wdq{b}", (128, 8, 2, 128))
                wdkv = sbuf(stA, f"wdkv{b}", (128, 8, 2, 128))
                wuq = sbuf(stA, f"wuq{b}", (128, 2, 8, 128))
                wuk = sbuf(stA, f"wuk{b}", (128, 2, 8, 128))
                wuv = sbuf(stA, f"wuv{b}", (128, 2, D))

                @block.gpsimd
                def _(g):
                    for nm, dst in [("WdqT", wdq), ("WdkvT", wdkv), ("WuqT", wuq),
                                    ("WukT", wuk), ("Wuv", wuv)]:
                        g.dma_start(out=dst[:], in_=hi[nm]).then_inc(LG.h, 16)
                        LG.inc(16)
                ev["mlaw", b] = LG.n

                for t in range(NTB):
                    xb = xfa if t % 2 == 0 else xfb
                    hb = hfa if t % 2 == 0 else hfb
                    @block.sync
                    def _(sy, t=t, xb=xb):
                        if t >= 2:
                            sy.wait_ge(VS.h, ev["x16v", b, t - 2])
                        sy.dma_start(out=xb[:], in_=x_in[b * S + t * 128:
                                                         b * S + (t + 1) * 128, :]
                                     ).then_inc(LD.h, 16)
                    ev["xld", b, t] = LD.inc(16)
                    @block.vector
                    def _(v, t=t, xb=xb, hb=hb):
                        v.wait_ge(LD.h, ev["xld", b, t])
                        if t >= 2:
                            v.wait_ge(LS.h, ev["x16s", b, t - 2])
                        v.tensor_copy(hb[:], xb[:]).then_inc(VS.h, 1)
                    ev["x16v", b, t] = VS.inc()
                    @block.scalar
                    def _(s, t=t, hb=hb):
                        s.wait_ge(VS.h, ev["x16v", b, t])
                        s.dma_start(out=d_x16[b * S + t * 128: b * S + (t + 1) * 128, :],
                                    in_=hb[:]).then_inc(LS.h, 16)
                    ev["x16s", b, t] = LS.inc(16)

                barrier(f"dbg_x16_{b}")
                @block.scalar
                def _(s):
                    s.wait_ge(LS.h, ev["x16s", b, NTB - 1])
                    for c in range(8):
                        s.dma_start_transpose(
                            xT[:, c, :],
                            d_x16[b * S:(b + 1) * S, c * 128:(c + 1) * 128]
                        ).then_inc(LS.h, 16)
                        LS.inc(16)
                ev["xT", b] = LS.n

                barrier(f"dbg_xT_{b}")
                for wi, (wsrc, cdst) in enumerate(((wdq, cqT), (wdkv, ckvT))):
                    for cc in range(2):
                        @block.tensor
                        def _(tn, wsrc=wsrc, cc=cc, wi=wi):
                            if wi == 0 and cc == 0:
                                tn.wait_ge(LS.h, ev["xT", b])
                                tn.wait_ge(LG.h, ev["mlaw", b])
                            else:
                                tn.wait_ge(VS.h, ev["cev", b, wi, cc - 1] if cc > 0
                                           else ev["cev", b, wi - 1, 1])
                            _cc = cc
                            for dc in range(8):
                                for qc in range(4):
                                    mm = tn.matmul(p_a[:, qc * 512:(qc + 1) * 512],
                                                   wsrc[:, dc, _cc, :],
                                                   xT[:, dc, qc * 512:(qc + 1) * 512],
                                                   start=(dc == 0), stop=(dc == 7))
                            mm.then_inc(TS.h, 1)
                        ev["cmm", b, wi, cc] = TS.inc()
                        @block.vector
                        def _(v, cdst=cdst, cc=cc, wi=wi):
                            v.wait_ge(TS.h, ev["cmm", b, wi, cc])
                            v.tensor_copy(cdst[:, cc, :], p_a[:]).then_inc(VS.h, 1)
                        ev["cev", b, wi, cc] = VS.inc()
                        barrier(f"dbg_cp_{wi}_{cc}_{b}")

                barrier(f"dbg_cproj_{b}")
                for wi, (wsrc, csrc, qdst) in enumerate(((wuq, cqT, QT), (wuk, ckvT, KT))):
                    for hc in range(8):
                        @block.tensor
                        def _(tn, wsrc=wsrc, csrc=csrc, hc=hc, wi=wi):
                            if wi == 0 and hc == 0:
                                tn.wait_ge(VS.h, ev["cev", b, 1, 1])
                            else:
                                tn.wait_ge(VS.h, ev["qev", b, wi, hc - 1] if hc > 0
                                           else ev["qev", b, wi - 1, 7])
                            for cc in range(2):
                                for qc in range(4):
                                    mm = tn.matmul(p_a[:, qc * 512:(qc + 1) * 512],
                                                   wsrc[:, cc, hc, :],
                                                   csrc[:, cc, qc * 512:(qc + 1) * 512],
                                                   start=(cc == 0), stop=(cc == 1))
                            mm.then_inc(TS.h, 1)
                        ev["qmm", b, wi, hc] = TS.inc()
                        @block.vector
                        def _(v, qdst=qdst, hc=hc, wi=wi):
                            v.wait_ge(TS.h, ev["qmm", b, wi, hc])
                            v.tensor_copy(qdst[:, hc, :], p_a[:]).then_inc(VS.h, 1)
                        ev["qev", b, wi, hc] = VS.inc()

                barrier(f"dbg_qk_{b}")
                @block.vector
                def _(v):
                    v.memset(Vp[:].rearrange("p t (h c) -> p (t h) c", c=65)
                             [:, :, 64:65], 1.0).then_inc(VS.h, 1)
                ev["vones", b] = VS.inc()
                for t in range(NTB):
                    @block.tensor
                    def _(tn, t=t):
                        if t == 0:
                            tn.wait_ge(VS.h, ev["qev", b, 1, 7])
                        else:
                            tn.wait_ge(VS.h, ev["vev", b, t - 1])
                        for cc in range(2):
                            for half in range(2):
                                mm = tn.matmul(p_a[:, half * 512:(half + 1) * 512],
                                               ckvT[:, cc, t * 128:(t + 1) * 128],
                                               wuv[:, cc, half * 512:(half + 1) * 512],
                                               start=(cc == 0), stop=(cc == 1))
                        mm.then_inc(TS.h, 1)
                    ev["vmm", b, t] = TS.inc()
                    @block.vector
                    def _(v, t=t):
                        v.wait_ge(TS.h, ev["vmm", b, t])
                        v.wait_ge(VS.h, ev["vones", b])
                        v.tensor_copy(
                            Vp[:, t, :].rearrange("p (h c) -> p h c", c=65)[:, :, 0:64],
                            p_a[:, 0:1024].rearrange("p (h c) -> p h c", c=64)).then_inc(VS.h, 1)
                    ev["vev", b, t] = VS.inc()

                barrier(f"dbg_proj_{b}")
                if _DBG_DEST and b == 0:
                    o_cq = nc.dram_tensor("o_cq", [128, S], BF16,
                                          kind="ExternalOutput").ap()
                    o_qt = nc.dram_tensor("o_qt", [64, S], BF16,
                                          kind="ExternalOutput").ap()
                    o_kt = nc.dram_tensor("o_kt", [64, S], BF16,
                                          kind="ExternalOutput").ap()
                    o_vp = nc.dram_tensor("o_vp", [128, 16 * 65], BF16,
                                          kind="ExternalOutput").ap()
                    @block.scalar
                    def _(s):
                        s.wait_ge(VS.h, ev["vev", b, NTB - 1])
                        s.dma_start(out=o_cq, in_=cqT[:, 0, :]).then_inc(LS.h, 16)
                        s.dma_start(out=o_qt, in_=QT[0:64, 0, :]).then_inc(LS.h, 16)
                        s.dma_start(out=o_kt, in_=KT[0:64, 0, :]).then_inc(LS.h, 16)
                        s.dma_start(out=o_vp, in_=Vp[:, 0, :]).then_inc(LS.h, 16)
                        LS.inc(64)
                    ev["mdump", b] = LS.n
                    barrier("dbg_mladump")
                stA.close()
                st2_ = contextlib.ExitStack()
                with st2_:
                  try:
                    p_av = psum(st2_, f"p_av{b}", (128, 2048))
                    PTt = sbuf(st2_, f"PTt{b}", (128, S))
                    attnT = sbuf(st2_, f"attnT{b}", (128, 8, S))
                    zrow = sbuf(st2_, f"zrow{b}", (1, S), F32)
                    Bt = sbuf(st2_, f"Bt{b}", (64, S))

                    for h in range(H):
                        hc, ho = h // 2, (h % 2) * 64
                        sb_exp = SS.n

                        @block.tensor
                        def _(tn, h=h, hc=hc, ho=ho, sb_exp=sb_exp):
                            if h == 0:
                                tn.wait_ge(VS.h, ev["vev", b, NTB - 1])
                            else:
                                tn.wait_ge(VS.h, ev["attn_ev", b, h - 1])
                            for kt in range(NTB):
                                qc0 = kt // 4
                                for qc in range(qc0, 4):
                                    tn.matmul(p_a[:, qc * 512:(qc + 1) * 512],
                                              KT[ho:ho + 64, hc, kt * 128:(kt + 1) * 128],
                                              QT[ho:ho + 64, hc, qc * 512:(qc + 1) * 512],
                                              start=True, stop=(qc != qc0))
                                tn.matmul(p_a[:, kt * 128:(kt + 1) * 128],
                                          iden16[:], maskdiag[:],
                                          start=False, stop=True).then_inc(TS.h, 1)
                                ev["smm", b, h, kt] = TS.inc()
                                tn.wait_ge(SS.h, sb_exp + kt + 1)
                                for qc in range(qc0, 4):
                                    lo = max(qc * 512, kt * 128)
                                    mm = tn.matmul(p_av[0:65, lo:(qc + 1) * 512],
                                                   Vp[:, kt, h * 65:(h + 1) * 65],
                                                   PTt[:, lo:(qc + 1) * 512],
                                                   start=(kt == 0),
                                                   stop=(kt == qc * 4 + 3))
                                if kt == NTB - 1:
                                    mm.then_inc(TS.h, 1)
                                    ev["avdone", b, h] = TS.inc()

                        @block.scalar
                        def _(s, h=h, sb_exp=sb_exp):
                            for kt in range(NTB):
                                s.wait_ge(TS.h, ev["smm", b, h, kt])
                                s.activation(PTt[:, kt * 128:], p_a[:, kt * 128:],
                                             AF.Exp, scale=0.125).then_inc(SS.h, 1)
                                SS.inc()

                        @block.vector
                        def _(v, h=h):
                            v.wait_ge(TS.h, ev["avdone", b, h])
                            v.reciprocal(zrow[:], p_av[64:65, :]).then_inc(VS.h, 1)
                        ev["zr", b, h] = VS.inc()

                        @block.tensor
                        def _(tn, h=h):
                            tn.wait_ge(VS.h, ev["zr", b, h])
                            for qc in range(4):
                                mm = tn.matmul(p_a[0:64, qc * 512:(qc + 1) * 512],
                                               ones_r[:, 0:64],
                                               zrow[:, qc * 512:(qc + 1) * 512],
                                               start=True, stop=True)
                            mm.then_inc(TS.h, 1)
                        ev["bc", b, h] = TS.inc()

                        @block.vector
                        def _(v, h=h, hc=hc, ho=ho):
                            v.wait_ge(TS.h, ev["bc", b, h])
                            v.tensor_copy(Bt[:], p_a[0:64, :])
                            v.tensor_tensor(attnT[ho:ho + 64, hc, :],
                                            p_av[0:64, :], Bt[:], ALU.mult).then_inc(VS.h, 1)
                        ev["attn_ev", b, h] = VS.inc()

                        if _DBG_AV and b == 0 and h == 0:
                            o_pt = nc.dram_tensor("o_pt", [128, S], BF16,
                                                  kind="ExternalOutput").ap()
                            o_av = nc.dram_tensor("o_av", [65, S], F32,
                                                  kind="ExternalOutput").ap()
                            o_zr = nc.dram_tensor("o_zr", [1, S], F32,
                                                  kind="ExternalOutput").ap()
                            avdump = sbuf(st2_, "avdump", (65, S), F32)
                            @block.vector
                            def _(v):
                                v.wait_ge(TS.h, ev["avdone", b, 0])
                                v.tensor_copy(avdump[:], p_av[0:65, :]).then_inc(VS.h, 1)
                            ev["avcp"] = VS.inc()
                            @block.scalar
                            def _(s):
                                s.wait_ge(VS.h, ev["avcp"])
                                s.wait_ge(VS.h, ev["attn_ev", b, 0])
                                s.dma_start(out=o_pt, in_=PTt[:]).then_inc(LS.h, 16)
                                s.dma_start(out=o_av, in_=avdump[:]).then_inc(LS.h, 16)
                                s.dma_start(out=o_zr, in_=zrow[:].bitcast(F32)
                                            ).then_inc(LS.h, 16)
                                LS.inc(48)
                            barrier("dbg_avdump")

                    if _DBG_AV and b == 0:
                        o_at = nc.dram_tensor("o_at", [128, S], BF16,
                                              kind="ExternalOutput").ap()
                        @block.scalar
                        def _(s):
                            s.wait_ge(VS.h, ev["attn_ev", b, H - 1])
                            s.dma_start(out=o_at, in_=attnT[:, 0, :]).then_inc(LS.h, 16)
                        ev["atdump"] = LS.inc(16)
                        barrier("dbg_atdump")

                    barrier(f"dbg_attn_{b}")
                    st3_ = contextlib.ExitStack()
                    with st3_:
                      try:
                          wo = sbuf(st3_, f"wo{b}", (128, 8, D))
                          @block.gpsimd
                          def _(g):
                              g.wait_ge(TS.h, ev["vmm", b, NTB - 1])
                              g.dma_start(out=wo[:], in_=hi["Wo"]).then_inc(LG.h, 16)
                          ev["wol", b] = LG.inc(16)

                          for t in range(NTB):
                              xb = xfa if t % 2 == 0 else xfb
                              @block.sync
                              def _(sy, t=t, xb=xb):
                                  if t >= 2:
                                      sy.wait_ge(VS.h, ev["x1", b, t - 2])
                                  sy.dma_start(out=xb[:],
                                               in_=x_in[b * S + t * 128:
                                                        b * S + (t + 1) * 128, :]
                                               ).then_inc(LD.h, 16)
                              ev["xld2", b, t] = LD.inc(16)

                              @block.tensor
                              def _(tn, t=t):
                                  if t == 0:
                                      tn.wait_ge(VS.h, ev["attn_ev", b, H - 1])
                                      tn.wait_ge(LG.h, ev["wol", b])
                                  else:
                                      tn.wait_ge(VS.h, ev["x1", b, t - 1])
                                  for hc in range(8):
                                      for half in range(2):
                                          mm = tn.matmul(
                                              p_av[:, half * 512:(half + 1) * 512],
                                              attnT[:, hc, t * 128:(t + 1) * 128],
                                              wo[:, hc, half * 512:(half + 1) * 512],
                                              start=(hc == 0), stop=(hc == 7))
                                  mm.then_inc(TS.h, 1)
                              ev["wo", b, t] = TS.inc()

                              @block.vector
                              def _(v, t=t, xb=xb):
                                  v.wait_ge(TS.h, ev["wo", b, t])
                                  v.wait_ge(LD.h, ev["xld2", b, t])
                                  if t >= 1:
                                      v.wait_ge(LS.h, ev["lns", b, t - 1])
                                  v.tensor_add(tfa[:], p_av[:, 0:1024],
                                               xb[:]).then_inc(VS.h, 1)
                                  VS.inc()
                                  ev["x1", b, t] = VS.n
                                  v.bn_stats(bn6[:, 0:6], tfa[:, 0:512]).then_inc(VS.h, 1)
                                  _b1 = VS.inc()
                                  v.bn_stats(bn6[:, 6:12], tfa[:, 512:1024]).then_inc(VS.h, 1)
                                  _b2 = VS.inc()
                                  v.wait_ge(VS.h, _b2)
                                  v.bn_aggr(bn2[:], bn6[:]).then_inc(VS.h, 1)
                                  VS.inc()
                              ev["bn1", b, t] = VS.n

                              if _DBG_LN and b == 0 and t == 0:
                                  o_x1 = nc.dram_tensor("o_x1", [128, D], F32,
                                                        kind="ExternalOutput").ap()
                                  o_bn = nc.dram_tensor("o_bn", [128, 2], F32,
                                                        kind="ExternalOutput").ap()
                                  @block.scalar
                                  def _(s):
                                      s.wait_ge(VS.h, ev["bn1", b, 0])
                                      s.dma_start(out=o_x1, in_=tfa[:]).then_inc(LS.h, 16)
                                      s.dma_start(out=o_bn, in_=bn2[:]).then_inc(LS.h, 16)
                                      LS.inc(32)
                                  barrier("dbg_lna")

                              @block.scalar
                              def _(s, t=t):
                                  s.wait_ge(VS.h, ev["bn1", b, t])
                                  s.activation(st3[:], bn2[:, 1:2], AF.Sqrt,
                                               bias=c_eps[:]).then_inc(SS.h, 1)
                              ev["sq1", b, t] = SS.inc()

                              @block.vector
                              def _(v, t=t):
                                  v.wait_ge(SS.h, ev["sq1", b, t])
                                  v.reciprocal(st2[:], st3[:])
                                  v.tensor_scalar(tfb[:], tfa[:], bn2[:, 0:1], st2[:],
                                                  ALU.subtract, ALU.mult)
                                  v.bn_stats(bn6[:, 0:6], tfb[:, 0:512]).then_inc(VS.h, 1)
                                  _b1 = VS.inc()
                                  v.bn_stats(bn6[:, 6:12], tfb[:, 512:1024]).then_inc(VS.h, 1)
                                  _b2 = VS.inc()
                                  v.wait_ge(VS.h, _b2)
                                  v.bn_aggr(bn2[:], bn6[:]).then_inc(VS.h, 1)
                              ev["bn2", b, t] = VS.inc()

                              @block.scalar
                              def _(s, t=t):
                                  s.wait_ge(VS.h, ev["bn2", b, t])
                                  s.activation(st3[:], bn2[:, 1:2], AF.Sqrt,
                                               scale=float(D) / (D - 1))
                                  s.activation(st3[:], st3[:], AF.Sqrt,
                                               bias=c_eps[:]).then_inc(SS.h, 1)
                              ev["sq2", b, t] = SS.inc()

                              @block.vector
                              def _(v, t=t):
                                  v.wait_ge(SS.h, ev["sq2", b, t])
                                  vw(v, v.reciprocal(st2[:], st3[:]))
                                  vw(v, v.tensor_scalar(tfa[:], tfb[:], bn2[:, 0:1], st2[:],
                                                        ALU.subtract, ALU.mult))
                                  v.tensor_copy(hfa[:], tfa[:]).then_inc(VS.h, 1)
                              ev["xn", b, t] = VS.inc()

                              if _DBG_LN and b == 0 and t == 0:
                                  o_x1n = nc.dram_tensor("o_x1n", [128, D], F32,
                                                         kind="ExternalOutput").ap()
                                  o_xn = nc.dram_tensor("o_xn", [128, D], F32,
                                                        kind="ExternalOutput").ap()
                                  o_st = nc.dram_tensor("o_st", [128, 4], F32,
                                                        kind="ExternalOutput").ap()
                                  @block.vector
                                  def _(v):
                                      v.wait_ge(VS.h, ev["xn", b, 0])
                                      v.tensor_copy(bn6[:, 0:1], st2[:])
                                      v.tensor_copy(bn6[:, 1:2], st3[:])
                                      v.tensor_copy(bn6[:, 2:4], bn2[:]).then_inc(VS.h, 1)
                                  _lnb = VS.inc()
                                  @block.scalar
                                  def _(s):
                                      s.wait_ge(VS.h, _lnb)
                                      s.dma_start(out=o_x1n, in_=tfb[:]).then_inc(LS.h, 16)
                                      s.dma_start(out=o_xn, in_=tfa[:]).then_inc(LS.h, 16)
                                      s.dma_start(out=o_st, in_=bn6[:, 0:4]).then_inc(LS.h, 16)
                                      LS.inc(48)
                                  barrier("dbg_lnb")

                              @block.scalar
                              def _(s, t=t):
                                  s.wait_ge(VS.h, ev["xn", b, t])
                                  r0 = b * S + t * 128
                                  s.dma_start(out=d_x1n[r0:r0 + 128, :],
                                              in_=tfb[:]).then_inc(LS.h, 16)
                                  s.dma_start(out=d_xn32[r0:r0 + 128, :],
                                              in_=tfa[:]).then_inc(LS.h, 16)
                                  s.dma_start(out=d_xn16[r0:r0 + 128, :],
                                              in_=hfa[:]).then_inc(LS.h, 16)
                                  LS.inc(48)
                              ev["lns", b, t] = LS.n
                      except _Cut:
                          _cut[0] = True
                    if _cut[0]:
                        raise _Cut()
                  except _Cut:
                    _cut[0] = True
                if _cut[0]:
                    raise _Cut()
              except _Cut:
                _cut[0] = True
            if _cut[0]:
                raise _Cut()
            barrier(f"mla{b}")

        try:
            if _zf_cut:
                raise _Cut()
            mla_batch(0)
            mla_batch(1)

            def routing_phase(names, srcdram):
                nf = len(names)
                tag = names[0]
                width = 768 if nf == 3 else 256
                st = contextlib.ExitStack()
                with st:
                    nwsb = sbuf(st, f"s_nw_{tag}", (128, 8, width), F32)
                    nbsb = sbuf(st, f"s_nb_{tag}", (1, width), F32)
                    Lbuf = sbuf(st, f"L_{tag}", (128, 8, width), F32)
                    selA = sbuf(st, f"selA_{tag}", (128, 8 * nf, 128), F32)
                    selB = sbuf(st, f"selB_{tag}", (128, 8 * nf, 64), F32)
                    dsb = sbuf(st, f"dsb_{tag}", (128, 8 * nf, 8), F32)
                    mskf2 = sbuf(st, f"mskf_{tag}", (128, 8 * nf, 128), F32)
                    dif = sbuf(st, f"dif_{tag}", (128, 8 * nf, 128), F32)
                    lball = sbuf(st, f"lba_{tag}", (128, 8 * nf), F32)
                    xr = [sbuf(st, f"xr_{tag}{i}", (128, 128), F32) for i in range(2)]
                    Gt = sbuf(st, f"Gt_{tag}", (128, NT, NG * nf), BF16)
                    prodall = sbuf(st, f"prod_{tag}", (128, NT, NG * nf), F32)
                    gtf = sbuf(st, f"gtf_{tag}", (128, NT * nf), F32)
                    iall = sbuf(st, f"iall_{tag}", (128, NT * nf), I32)
                    rkall = sbuf(st, f"rk_{tag}", (128, NT * nf), F32)
                    cnall = sbuf(st, f"cn_{tag}", (128, NT * nf), F32)
                    dsi = sbuf(st, f"dsi_{tag}", (128, NT * nf), I32)
                    p_t = psum(st, f"pt_{tag}", (128, 128))
                    p_l = psum(st, f"pl_{tag}", (128, width))
                    p_r = psum(st, f"pr_{tag}", (128, NG * nf))

                    @block.gpsimd
                    def _(g):
                        g.dma_start(out=nwsb[:], in_=hi["nwT_qkv" if nf == 3 else "nwT_f"]
                                    ).then_inc(LG.h, 16)
                        g.dma_start(out=nbsb[:], in_=hi["nb_qkv" if nf == 3 else "nb_f"]
                                    ).then_inc(LG.h, 16)
                        LG.inc(32)
                    ev["nwl", tag] = LG.n

                    @block.vector
                    def _(v):
                        v.wait_ge(LG.h, ev["nwl", tag])
                        v.tensor_copy(fence[:], pidx[:]).then_inc(VS.h, 1)
                    ev["nwpr", tag] = VS.inc()

                    for ch in range(4):
                        for tt in range(8):
                            t = ch * 8 + tt
                            xb = xfa if t % 2 == 0 else xfb
                            @block.sync
                            def _(sy, t=t, xb=xb):
                                if t >= 2:
                                    sy.wait_ge(TS.h, ev["rtp", tag, t - 2, 7])
                                sy.dma_start(out=xb[:],
                                             in_=srcdram[t * 128:(t + 1) * 128, :]
                                             ).then_inc(LD.h, 16)
                            ev["rx", tag, t] = LD.inc(16)

                            base_vs = VS.n
                            @block.tensor
                            def _(tn, t=t, xb=xb, width=width, base_vs=base_vs):
                                tn.wait_ge(LD.h, ev["rx", tag, t])
                                if t == 0:
                                    tn.wait_ge(VS.h, ev["nwpr", tag])
                                else:
                                    tn.wait_ge(VS.h, ev["lev", tag, t - 1])
                                for dc in range(8):
                                    tn.transpose(p_t[:], xb[:, dc * 128:(dc + 1) * 128],
                                                 iden[:]).then_inc(TS.h, 1)
                                    ev["rtp", tag, t, dc] = TS.inc()
                                    tn.wait_ge(VS.h, base_vs + dc + 1)
                                    if width == 768:
                                        tn.matmul(p_l[:, 0:512], xr[dc % 2][:],
                                                  nwsb[:, dc, 0:512],
                                                  start=(dc == 0), stop=False)
                                        tn.matmul(p_l[:, 512:768], xr[dc % 2][:],
                                                  nwsb[:, dc, 512:768],
                                                  start=(dc == 0), stop=False)
                                    else:
                                        tn.matmul(p_l[:, 0:256], xr[dc % 2][:],
                                                  nwsb[:, dc, 0:256],
                                                  start=(dc == 0), stop=False)
                                if width == 768:
                                    tn.matmul(p_l[:, 0:512], ones_r[:], nbsb[:, 0:512],
                                              start=False, stop=True)
                                    mm = tn.matmul(p_l[:, 512:768], ones_r[:],
                                                   nbsb[:, 512:768], start=False, stop=True)
                                else:
                                    mm = tn.matmul(p_l[:, 0:256], ones_r[:], nbsb[:, 0:256],
                                                   start=False, stop=True)
                                mm.then_inc(TS.h, 1)
                            ev["rl", tag, t] = TS.inc()

                            @block.vector
                            def _(v, t=t, tt=tt, width=width, base_vs=base_vs):
                                for dc in range(8):
                                    v.wait_ge(TS.h, ev["rtp", tag, t, dc])
                                    v.tensor_copy(xr[dc % 2][:], p_t[:]).then_inc(VS.h, 1)
                                    VS.inc()
                                assert VS.n == base_vs + 8
                                v.wait_ge(TS.h, ev["rl", tag, t])
                                v.tensor_copy(Lbuf[:, tt, 0:width], p_l[:, 0:width]).then_inc(VS.h, 1)
                            ev["lev", tag, t] = VS.inc()

                        @block.vector
                        def _(v, ch=ch, nf=nf, names=names, width=width):
                            nb_ = 8 * nf
                            blkw = width // nf
                            L3 = Lbuf[:].rearrange("p t (f w) -> p (t f) w", w=blkw)
                            v.wait_ge(VS.h, ev["lev", tag, ch * 8 + 7])
                            W = 127
                            cur = None
                            for lvl in range(8):
                                src = L3[:, 0:nb_, :] if lvl == 0 else cur
                                vw(v, v.tensor_single_scalar(dsb[:, 0:nb_, lvl:lvl + 1],
                                                             src[:, :, 0:1], 0.0, ALU.is_gt))
                                if W >= 1:
                                    dstb = (selA if lvl % 2 == 0 else selB)[:, 0:nb_, 0:W]
                                    hi = src[:, :, 1 + W:1 + 2 * W]
                                    lo = src[:, :, 1:1 + W]
                                    v.tensor_copy(
                                        mskf2[:, 0:nb_, 0:W],
                                        dsb[:, 0:nb_, lvl:lvl + 1].broadcast_to((128, nb_, W)))
                                    vw(v, v.tensor_tensor(dif[:, 0:nb_, 0:W], hi, lo,
                                                          ALU.subtract))
                                    vw(v, v.tensor_tensor(dif[:, 0:nb_, 0:W],
                                                          dif[:, 0:nb_, 0:W],
                                                          mskf2[:, 0:nb_, 0:W], ALU.mult))
                                    vw(v, v.tensor_tensor(dstb, dif[:, 0:nb_, 0:W], lo,
                                                          ALU.add))
                                    cur = dstb
                                W = (W - 1) // 2
                            vw(v, v.tensor_tensor(dsb[:, 0:nb_, :], dsb[:, 0:nb_, :],
                                                  pw[:, 0:nb_, :], ALU.mult))
                            vw(v, v.tensor_reduce(lball[:, 0:nb_], dsb[:, 0:nb_, :],
                                                  AXX, ALU.add))
                            for i, n in enumerate(names):
                                vw(v, v.tensor_copy(
                                    leafb[n][:, ch * 8:(ch + 1) * 8],
                                    lball[:, 0:nb_].rearrange("p (t f) -> p t f",
                                                              f=nf)[:, :, i]))
                            v.tensor_copy(fence[:], pidx[:]).then_inc(VS.h, 1)
                        ev["half", tag, ch] = VS.inc()

                    if _DBG_DEST:
                        d_dlog = nc.dram_tensor(f"dlog_{tag}", [128, 8 * width], F32,
                                                kind="ExternalOutput").ap()
                        @block.sync
                        def _(sy):
                            sy.wait_ge(VS.h, ev["half", tag, 3])
                            sy.dma_start(out=d_dlog,
                                         in_=Lbuf[:].rearrange("p a b -> p (a b)")
                                         ).then_inc(LD.h, 16)
                        ev["dlog", tag] = LD.inc(16)
                        @block.vector
                        def _(v):
                            v.wait_ge(LD.h, ev["dlog", tag])
                            v.tensor_copy(fence[:], pidx[:]).then_inc(VS.h, 1)
                        VS.inc()

                    @block.vector
                    def _(v, names=names, nf=nf):
                        v.wait_ge(VS.h, ev["half", tag, 3])
                        i3 = iall[:].rearrange("p (t f) -> p t f", f=nf)
                        g3 = gtf[:].rearrange("p (t f) -> p t f", f=nf)
                        for i, n in enumerate(names):
                            vw(v, v.tensor_scalar(g3[:, :, i], leafb[n][:], 1.0 / NG,
                                                  0.5 / NG, ALU.mult, ALU.add))
                            vw(v, v.tensor_copy(i3[:, :, i], g3[:, :, i]))
                            vw(v, v.tensor_copy(bnd[n][:], i3[:, :, i]))
                            vw(v, v.tensor_tensor(g3[:, :, i], g3[:, :, i], bnd[n][:],
                                                  ALU.is_lt))
                            vw(v, v.tensor_tensor(bnd[n][:], bnd[n][:], g3[:, :, i],
                                                  ALU.subtract))
                            vw(v, v.scalar_tensor_tensor(g3[:, :, i], bnd[n][:],
                                                         -float(NG), leafb[n][:],
                                                         ALU.mult, ALU.add))
                        v.tensor_copy(fence[:], pidx[:]).then_inc(VS.h, 1)
                    ev["bandg", tag] = VS.inc()

                    for t in range(NT):
                        @block.vector
                        def _(v, t=t, nf=nf):
                            if t == 0:
                                v.wait_ge(VS.h, ev["bandg", tag])
                            for i in range(nf):
                                v.tensor_scalar(Gt[:, t, i * NG:(i + 1) * NG], iota37[:],
                                                gtf[:, t * nf + i:t * nf + i + 1],
                                                None, ALU.is_equal)
                            v.tensor_copy(fence[:], pidx[:]).then_inc(VS.h, 1)
                        ev["gt", tag, t] = VS.inc()

                        @block.tensor
                        def _(tn, t=t, nf=nf):
                            tn.wait_ge(VS.h, ev["gt", tag, t])
                            if t >= 1:
                                tn.wait_ge(VS.h, ev["dst", tag, t - 1])
                            mm = tn.matmul(p_r[:, 0:NG * nf], tristrict[:],
                                           Gt[:, t, 0:NG * nf], start=True, stop=(t == 0))
                            for j in range(t):
                                mm = tn.matmul(p_r[:, 0:NG * nf], allones[:],
                                               Gt[:, j, 0:NG * nf],
                                               start=False, stop=(j == t - 1))
                            mm.then_inc(TS.h, 1)
                        ev["rk1", tag, t] = TS.inc()

                        @block.vector
                        def _(v, t=t, nf=nf):
                            v.wait_ge(TS.h, ev["rk1", tag, t])
                            v.tensor_tensor(prodall[:, t, 0:NG * nf], Gt[:, t, 0:NG * nf],
                                            p_r[:, 0:NG * nf], ALU.mult).then_inc(VS.h, 1)
                        ev["dst", tag, t] = VS.inc()

                    @block.vector
                    def _(v, names=names, nf=nf):
                        v.wait_ge(VS.h, ev["dst", tag, NT - 1])
                        nw_ = NT * nf
                        vw(v, v.tensor_reduce(
                            rkall[:, 0:nw_],
                            prodall[:].rearrange("p t (f g) -> p (t f) g", g=NG),
                            AXX, ALU.add))
                        vw(v, v.tensor_single_scalar(cnall[:, 0:nw_], rkall[:, 0:nw_],
                                                     float(CAP), ALU.is_lt))
                        vw(v, v.scalar_tensor_tensor(rkall[:, 0:nw_], gtf[:, 0:nw_],
                                                     float(CAP), rkall[:, 0:nw_],
                                                     ALU.mult, ALU.add))
                        vw(v, v.tensor_scalar_add(rkall[:, 0:nw_], rkall[:, 0:nw_],
                                                  -float(TRASH)))
                        vw(v, v.tensor_tensor(rkall[:, 0:nw_], rkall[:, 0:nw_],
                                              cnall[:, 0:nw_], ALU.mult))
                        vw(v, v.tensor_scalar_add(rkall[:, 0:nw_], rkall[:, 0:nw_],
                                                  float(TRASH)))
                        vw(v, v.tensor_copy(dsi[:, 0:nw_], rkall[:, 0:nw_]))
                        d3 = dsi[:].rearrange("p (t f) -> p t f", f=nf)
                        for i, n in enumerate(names):
                            vw(v, v.tensor_copy(dest_i[n][:], d3[:, :, i]))
                        v.tensor_copy(fence[:], pidx[:]).then_inc(VS.h, 1)
                    ev["dstw", tag] = VS.inc()
                barrier(f"rt{tag}")

            routing_phase(["q", "k", "v"], d_xn32)

            if _DBG_DEST:
                d_dbgx = nc.dram_tensor("dbgx", [128, 2 * D], F32,
                                        kind="ExternalOutput").ap()
                @block.sync
                def _(sy):
                    sy.dma_start(out=xfa[:], in_=d_xn32[0:128, :]).then_inc(LD.h, 16)
                    sy.dma_start(out=xfb[:], in_=d_x1n[0:128, :]).then_inc(LD.h, 16)
                ev["dbgxl"] = LD.inc(32)
                @block.scalar
                def _(s):
                    s.wait_ge(LD.h, ev["dbgxl"])
                    s.dma_start(out=d_dbgx[:, 0:D], in_=xfa[:]).then_inc(LD.h, 16)
                    s.dma_start(out=d_dbgx[:, D:2 * D], in_=xfb[:]).then_inc(LD.h, 16)
                ev["dbgxs"] = LD.inc(32)
                @block.vector
                def _(v):
                    v.wait_ge(LD.h, ev["dbgxs"])
                    v.tensor_copy(fence[:], pidx[:]).then_inc(VS.h, 1)
                VS.inc()
                d_dbg = nc.dram_tensor("dbg", [128, 6 * NT], F32,
                                       kind="ExternalOutput").ap()
                dbf = sbuf(root, "dbf", (128, 3 * NT), F32)
                @block.vector
                def _(v):
                    for i, n in enumerate("qkv"):
                        v.tensor_copy(dbf[:, i * NT:(i + 1) * NT], dest_i[n][:])
                    v.tensor_copy(fence[:], pidx[:]).then_inc(VS.h, 1)
                ev["dbgc"] = VS.inc()
                @block.sync
                def _(sy):
                    sy.wait_ge(VS.h, ev["dbgc"])
                    sy.dma_start(out=d_dbg[:, 0:3 * NT], in_=dbf[:]).then_inc(LD.h, 16)
                    for i, n in enumerate("qkv"):
                        sy.dma_start(out=d_dbg[:, (3 + i) * NT:(4 + i) * NT],
                                     in_=leafb[n][:]).then_inc(LD.h, 16)
                    LD.inc(64)
                barrier("dbg_destdump")

            def legs_phase(name, src16, act_fn, outdram):
                st = contextlib.ExitStack()
                _cut = [False]
                with st:
                  try:
                      extt = [sbuf(st, f"ext{name}{i}", (128, EXT)) for i in range(2)]
                      xTs = [sbuf(st, f"xTs{name}{i}", (128, 9, CHG * CAP)) for i in range(2)]
                      w1c = [sbuf(st, f"w1c{name}{i}", (128, CHG, 8, GRP * 16))
                             for i in range(2)]
                      w2c = [sbuf(st, f"w2c{name}{i}", (128, CHG, D)) for i in range(2)]
                      b1c = [sbuf(st, f"b1c{name}{i}", (1, CHG * GRP * 16)) for i in range(2)]
                      hsb = sbuf(st, f"hsb{name}", (128, CAP))
                      msb = sbuf(st, f"msb{name}", (128, CAP))
                      hpk = [sbuf(st, f"hpk{name}{i}", (128, CAP)) for i in range(2)]
                      sot = [sbuf(st, f"sot{name}{i}", (128, D)) for i in range(2)]
                      cant = [sbuf(st, f"cant{name}{i}", (128, D)) for i in range(2)]
                      p_w1 = psum(st, f"pw1{name}", (112, CAP))
                      p_o = psum(st, f"po{name}", (128, D))

                      @block.vector
                      def _(v):
                          v.memset(hpk[0][:], 0.0)
                          v.memset(hpk[1][:], 0.0)
                          v.memset(hsb[96:128, :], 1.0)
                          v.tensor_copy(fence[:], pidx[:]).then_inc(VS.h, 1)
                      ev["hz", name] = VS.inc()

                      for t in range(NT):
                          hb = hfa if t % 2 == 0 else hfb
                          @block.sync
                          def _(sy, t=t, hb=hb):
                              if t >= 2:
                                  sy.wait_ge(VS.h, ev["ex", name, t - 2])
                              sy.dma_start(out=hb[:], in_=src16[t * 128:(t + 1) * 128, :]
                                           ).then_inc(LD.h, 16)
                          ev["sx", name, t] = LD.inc(16)

                          @block.vector
                          def _(v, t=t, hb=hb):
                              v.wait_ge(LD.h, ev["sx", name, t])
                              if t >= 2:
                                  v.wait_ge(LG.h, ev["sc", name, t - 2])
                              e = extt[t % 2]
                              v.tensor_copy(e[:, 0:D], hb[:])
                              vw(v, v.tensor_copy(
                                  tfb[:, 0:128],
                                  bnd[name][:, t:t + 1].broadcast_to((128, 128))))
                              v.tensor_copy(e[:, D:EXT], tfb[:, 0:128]).then_inc(VS.h, 1)
                          ev["ex", name, t] = VS.inc()

                          @block.gpsimd
                          def _(g, t=t):
                              g.wait_ge(VS.h, ev["ex", name, t])
                              g.indirect_dma_start(
                                  d_sortx, IndirectOffsetOnAxis(dest_i[name][:, t:t + 1], 0),
                                  extt[t % 2][:], None).then_inc(LG.h, 16)
                          ev["sc", name, t] = LG.inc(16)

                      barrier(f"dbg_lsc_{name}")

                      w1done, w2done, actdone, soevs, sosts = [], [], [], {}, {}
                      gidx = 0
                      for ch in range(NCH):
                          g0 = ch * CHG
                          ng = min(CHG, NG - g0)
                          bufi = ch % 2
                          @block.scalar
                          def _(s, ch=ch, g0=g0, ng=ng, bufi=bufi):
                              if ch == 0:
                                  s.wait_ge(LG.h, ev["sc", name, NT - 1])
                              if ch >= 2:
                                  s.wait_ge(TS.h, w1done[-1])
                                  gprev = (ch - 2) * CHG + min(CHG, NG - (ch - 2) * CHG) - 1
                                  s.wait_ge(VS.h, ev["hp", name, gprev])
                              for strip in range(9):
                                  s.dma_start_transpose(
                                      xTs[bufi][:, strip, 0:ng * CAP],
                                      d_sortx[g0 * CAP:g0 * CAP + ng * CAP,
                                              strip * 128:(strip + 1) * 128]
                                  ).then_inc(LS.h, 16)
                                  LS.inc(16)
                          ev["xts", name, ch] = LS.n

                          @block.gpsimd
                          def _(g, ch=ch, g0=g0, ng=ng, bufi=bufi):
                              if ch >= 2:
                                  g.wait_ge(TS.h, w2done[-1])
                              g.dma_start(out=w1c[bufi][:, 0:ng, :, :],
                                          in_=hi[f"W1_{name}"][:, g0:g0 + ng, :, :]
                                          ).then_inc(LG.h, 16)
                              g.dma_start(out=w2c[bufi][:, 0:ng, :],
                                          in_=hi[f"W2_{name}"][:, g0:g0 + ng, :]
                                          ).then_inc(LG.h, 16)
                              g.dma_start(out=b1c[bufi][:, 0:ng * GRP * 16],
                                          in_=hi[f"b1_{name}"][:, g0 * GRP * 16:
                                                               (g0 + ng) * GRP * 16]
                                          ).then_inc(LG.h, 16)
                              LG.inc(48)
                          ev["wl", name, ch] = LG.n

                          for gi in range(ng):
                              hb_ = hpk[gidx % 2]
                              @block.tensor
                              def _(tn, ch=ch, gi=gi, bufi=bufi):
                                  if gi == 0:
                                      tn.wait_ge(LS.h, ev["xts", name, ch])
                                      tn.wait_ge(LG.h, ev["wl", name, ch])
                                      if ch == 0:
                                          tn.wait_ge(VS.h, ev["hz", name])
                                  if actdone:
                                      tn.wait_ge(SS.h, actdone[-1])
                                  c0 = gi * CAP
                                  for dc in range(8):
                                      tn.matmul(p_w1[:], w1c[bufi][:, gi, dc, :],
                                                xTs[bufi][:, dc, c0:c0 + CAP],
                                                start=(dc == 0), stop=False)
                                  tn.matmul(p_w1[:], b1c[bufi][0:1, gi * 112:(gi + 1) * 112],
                                            ones_320[:], start=False,
                                            stop=True).then_inc(TS.h, 1)
                              w1done.append(TS.inc())

                              @block.scalar
                              def _(s, act_fn=act_fn):
                                  s.wait_ge(TS.h, w1done[-1])
                                  s.activation(hsb[0:112, :], p_w1[:], act_fn).then_inc(SS.h, 1)
                              actdone.append(SS.inc())

                              @block.vector
                              def _(v, ch=ch, gi=gi, bufi=bufi, hb_=hb_, gidx=gidx):
                                  c0 = gi * CAP
                                  v.wait_ge(SS.h, actdone[-1])
                                  if gidx >= 2:
                                      v.wait_ge(TS.h, w2done[gidx - 2])
                                  vw(v, v.tensor_scalar(msb[:], xTs[bufi][:, 8, c0:c0 + CAP],
                                                        cmpv[:], None, ALU.is_equal))
                                  v.tensor_tensor(hb_[:], hsb[:], msb[:], ALU.mult).then_inc(VS.h, 1)
                              ev["hp", name, gidx] = VS.inc()

                              for sbk in range(3):
                                  w = 128 if sbk < 2 else CAP - 256
                                  @block.tensor
                                  def _(tn, ch=ch, gi=gi, bufi=bufi, sbk=sbk, w=w,
                                        hb_=hb_, gidx=gidx):
                                      if sbk == 0:
                                          tn.wait_ge(VS.h, ev["hp", name, gidx])
                                      if soevs:
                                          tn.wait_ge(VS.h, soevs[max(soevs)])
                                      for half in range(2):
                                          mm = tn.matmul(
                                              p_o[0:w, half * 512:(half + 1) * 512],
                                              hb_[:, sbk * 128:sbk * 128 + w],
                                              w2c[bufi][:, gi, half * 512:(half + 1) * 512],
                                              start=True, stop=True)
                                      mm.then_inc(TS.h, 1)
                                  ev["w2", name, gidx, sbk] = TS.inc()
                                  if sbk == 2:
                                      w2done.append(ev["w2", name, gidx, 2])

                                  @block.vector
                                  def _(v, gidx=gidx, sbk=sbk, w=w):
                                      v.wait_ge(TS.h, ev["w2", name, gidx, sbk])
                                      if (gidx, sbk - 2) in sosts:
                                          v.wait_ge(LS.h, sosts[(gidx, sbk - 2)])
                                      elif sbk < 2 and (gidx - 1, sbk + 1) in sosts:
                                          v.wait_ge(LS.h, sosts[(gidx - 1, sbk + 1)])
                                      v.tensor_copy(sot[sbk % 2][0:w, :], p_o[0:w, :]).then_inc(VS.h, 1)
                                  soevs[(gidx, sbk)] = VS.inc()

                                  @block.scalar
                                  def _(s, ch=ch, gi=gi, g0=g0, gidx=gidx, sbk=sbk, w=w):
                                      s.wait_ge(VS.h, soevs[(gidx, sbk)])
                                      r0 = (g0 + gi) * CAP + sbk * 128
                                      s.dma_start(out=d_sorto[r0:r0 + w, :],
                                                  in_=sot[sbk % 2][0:w, :]).then_inc(LS.h, 16)
                                  sosts[(gidx, sbk)] = LS.inc(16)
                              gidx += 1
                          barrier(f"dbg_lch_{name}_{ch}")

                      for t in range(NT):
                          @block.gpsimd
                          def _(g, t=t):
                              if t == 0:
                                  g.wait_ge(LS.h, sosts[(NG - 1, 2)])
                              if t >= 2:
                                  g.wait_ge(LS.h, ev["cst", name, t - 2])
                              g.indirect_dma_start(
                                  cant[t % 2][:], None, d_sorto,
                                  IndirectOffsetOnAxis(dest_i[name][:, t:t + 1], 0)
                              ).then_inc(LG.h, 16)
                          ev["cg", name, t] = LG.inc(16)

                          @block.scalar
                          def _(s, t=t):
                              s.wait_ge(LG.h, ev["cg", name, t])
                              s.dma_start(out=outdram[t * 128:(t + 1) * 128, :],
                                          in_=cant[t % 2][:]).then_inc(LS.h, 16)
                          ev["cst", name, t] = LS.inc(16)
                  except _Cut:
                      _cut[0] = True
                if _cut[0]:
                    raise _Cut()
                barrier(f"legs{name}")

            legs_phase("v", d_xn16, AF.Relu, d_vcan)
            legs_phase("k", d_xn16, AF.Relu, d_kcan)
            legs_phase("q", d_xn16, AF.Relu, d_qcan)

            def ffn_attn(b):
                st = contextlib.ExitStack()
                with st:
                    qT2 = sbuf(st, f"qT2{b}", (128, 8, S))
                    kT2 = sbuf(st, f"kT2{b}", (128, 8, S))
                    PT2 = sbuf(st, f"PT2{b}", (128, NTB, S))
                    vtok = sbuf(st, f"vtok{b}", (128, NTB, D))
                    p_s = psum(st, f"ps{b}", (128, 2048))
                    p_o = psum(st, f"po2{b}", (128, D))
                    p_1 = psum(st, f"p1{b}", (128, 1))

                    @block.scalar
                    def _(s):
                        for c in range(8):
                            s.dma_start_transpose(
                                qT2[:, c, :], d_qcan[b * S:(b + 1) * S,
                                                     c * 128:(c + 1) * 128]
                            ).then_inc(LS.h, 16)
                            s.dma_start_transpose(
                                kT2[:, c, :], d_kcan[b * S:(b + 1) * S,
                                                     c * 128:(c + 1) * 128]
                            ).then_inc(LS.h, 16)
                            LS.inc(32)
                    ev["qkT", b] = LS.n

                    @block.sync
                    def _(sy):
                        sy.dma_start(
                            out=vtok[:],
                            in_=d_vcan[b * S:(b + 1) * S, :]
                            .rearrange("(t p) d -> p t d", p=128)).then_inc(LD.h, 16)
                    ev["vl", b] = LD.inc(16)

                    for kt in range(NTB):
                        @block.tensor
                        def _(tn, kt=kt):
                            if kt == 0:
                                tn.wait_ge(LS.h, ev["qkT", b])
                            else:
                                tn.wait_ge(SS.h, ev["fexp", b, kt - 1])
                            for dc in range(8):
                                for qc in range(4):
                                    mm = tn.matmul(p_s[:, qc * 512:(qc + 1) * 512],
                                                   kT2[:, dc, kt * 128:(kt + 1) * 128],
                                                   qT2[:, dc, qc * 512:(qc + 1) * 512],
                                                   start=(dc == 0), stop=(dc == 7))
                            mm.then_inc(TS.h, 1)
                        ev["fs", b, kt] = TS.inc()

                        @block.scalar
                        def _(s, kt=kt):
                            s.wait_ge(TS.h, ev["fs", b, kt])
                            s.activation(PT2[:, kt, :], p_s[:], AF.Exp,
                                         scale=1.0 / 32.0).then_inc(SS.h, 1)
                        ev["fexp", b, kt] = SS.inc()

                    for qt in range(NTB):
                        xb = xfa if qt % 2 == 0 else xfb
                        @block.sync
                        def _(sy, qt=qt, xb=xb):
                            if qt >= 2:
                                sy.wait_ge(VS.h, ev["y", b, qt - 2])
                            sy.dma_start(out=xb[:],
                                         in_=d_xn32[b * S + qt * 128:
                                                    b * S + (qt + 1) * 128, :]
                                         ).then_inc(LD.h, 16)
                        ev["xnl", b, qt] = LD.inc(16)

                        @block.tensor
                        def _(tn, qt=qt):
                            if qt == 0:
                                tn.wait_ge(SS.h, ev["fexp", b, NTB - 1])
                                tn.wait_ge(LD.h, ev["vl", b])
                            else:
                                tn.wait_ge(VS.h, ev["y", b, qt - 1])
                            for kt in range(NTB):
                                tn.matmul(p_1[:], PT2[:, kt, qt * 128:(qt + 1) * 128],
                                          ones_col[:], start=(kt == 0),
                                          stop=(kt == NTB - 1))
                                for half in range(2):
                                    mm = tn.matmul(p_o[:, half * 512:(half + 1) * 512],
                                                   PT2[:, kt, qt * 128:(qt + 1) * 128],
                                                   vtok[:, kt, half * 512:(half + 1) * 512],
                                                   start=(kt == 0), stop=(kt == NTB - 1))
                            mm.then_inc(TS.h, 1)
                        ev["fav", b, qt] = TS.inc()

                        @block.vector
                        def _(v, qt=qt, xb=xb):
                            v.wait_ge(TS.h, ev["fav", b, qt])
                            v.wait_ge(LD.h, ev["xnl", b, qt])
                            if qt >= 1:
                                v.wait_ge(LS.h, ev["ys", b, qt - 1])
                            vw(v, v.reciprocal(st1[:], p_1[:]))
                            vw(v, v.scalar_tensor_tensor(tfa[:], p_o[:], st1[:], xb[:],
                                                         ALU.mult, ALU.add))
                            v.tensor_copy(hfa[:], tfa[:]).then_inc(VS.h, 1)
                        ev["y", b, qt] = VS.inc()

                        @block.scalar
                        def _(s, qt=qt):
                            s.wait_ge(VS.h, ev["y", b, qt])
                            r0 = b * S + qt * 128
                            s.dma_start(out=d_y32[r0:r0 + 128, :],
                                        in_=tfa[:]).then_inc(LS.h, 16)
                            s.dma_start(out=d_y16[r0:r0 + 128, :],
                                        in_=hfa[:]).then_inc(LS.h, 16)
                            LS.inc(32)
                        ev["ys", b, qt] = LS.n
                barrier(f"fattn{b}")

            ffn_attn(0)
            ffn_attn(1)

            routing_phase(["f"], d_y32)
            legs_phase("f", d_y16, AF.Gelu, d_qcan)

            for t in range(NT):
                xb = xfa if t % 2 == 0 else xfb
                tb = tfa if t % 2 == 0 else tfb
                hb = hfa if t % 2 == 0 else hfb
                @block.sync
                def _(sy, t=t, xb=xb, tb=tb, hb=hb):
                    if t >= 2:
                        sy.wait_ge(VS.h, ev["z", t - 2])
                    sy.dma_start(out=xb[:], in_=d_x1n[t * 128:(t + 1) * 128, :]
                                 ).then_inc(LD.h, 16)
                    sy.dma_start(out=tb[:], in_=d_y32[t * 128:(t + 1) * 128, :]
                                 ).then_inc(LD.h, 16)
                    sy.dma_start(out=hb[:], in_=d_qcan[t * 128:(t + 1) * 128, :]
                                 ).then_inc(LD.h, 16)
                ev["fl", t] = LD.inc(48)

                @block.vector
                def _(v, t=t, xb=xb, tb=tb, hb=hb):
                    v.wait_ge(LD.h, ev["fl", t])
                    if t >= 2:
                        v.wait_ge(OUT.h, ev["os", t - 2])
                    v.tensor_add(xb[:], xb[:], tb[:])
                    v.tensor_tensor(xb[:], xb[:], hb[:], ALU.add)
                    v.bn_stats(bn6[:, 0:6], xb[:, 0:512]).then_inc(VS.h, 1)
                    _b1 = VS.inc()
                    v.bn_stats(bn6[:, 6:12], xb[:, 512:1024]).then_inc(VS.h, 1)
                    _b2 = VS.inc()
                    v.wait_ge(VS.h, _b2)
                    v.bn_aggr(bn2[:], bn6[:]).then_inc(VS.h, 1)
                ev["zbn", t] = VS.inc()

                @block.scalar
                def _(s, t=t):
                    s.wait_ge(VS.h, ev["zbn", t])
                    s.activation(st3[:], bn2[:, 1:2], AF.Sqrt, bias=c_eps[:]).then_inc(SS.h, 1)
                ev["zsq", t] = SS.inc()

                @block.vector
                def _(v, t=t, xb=xb, tb=tb):
                    v.wait_ge(SS.h, ev["zsq", t])
                    v.reciprocal(st2[:], st3[:])
                    v.tensor_scalar(tb[:], xb[:], bn2[:, 0:1], st2[:],
                                    ALU.subtract, ALU.mult).then_inc(VS.h, 1)
                ev["z", t] = VS.inc()

                @block.scalar
                def _(s, t=t, tb=tb):
                    s.wait_ge(VS.h, ev["z", t])
                    s.dma_start(out=out_d[t * 128:(t + 1) * 128, :],
                                in_=tb[:]).then_inc(OUT.h, 16)
                ev["os", t] = OUT.inc(16)

            @block.gpsimd
            def _(g):
                g.wait_ge(OUT.h, OUT.n)
        except _Cut:
            pass
        if upto is not None:
            o_cst = nc.dram_tensor("o_cst", [128, 6 * 128], F32,
                                   kind="ExternalOutput").ap()
            o_dst = nc.dram_tensor("o_dst", [128, 4 * NT], F32,
                                   kind="ExternalOutput").ap()
            @block.vector
            def _(v):
                v.memset(tfa[:], 0.0)
                v.tensor_copy(tfb[:, 0:128], iden16[:])
                v.tensor_copy(tfb[:, 128:256], maskdiag[:])
                v.tensor_copy(tfb[:, 256:384], tristrict[:])
                v.tensor_copy(tfb[:, 384:512], allones[:])
                v.tensor_copy(tfb[:, 512:640], upperincl[:])
                v.tensor_copy(tfb[:, 640:768], iden[:])
                for i, n in enumerate("qkvf"):
                    v.tensor_copy(xfa[:, i * NT:(i + 1) * NT], dest_i[n][:])
                v.tensor_copy(fence[:], pidx[:]).then_inc(VS.h, 1)
            _zout = VS.inc()
            @block.scalar
            def _(sc):
                sc.wait_ge(GS.h, GS.n)
                sc.wait_ge(VS.h, _zout)
                sc.dma_start(out=o_cst, in_=tfb[:, 0:768]).then_inc(OUT.h, 16)
                sc.dma_start(out=o_dst, in_=xfa[:, 0:4 * NT]).then_inc(OUT.h, 16)
                OUT.inc(32)
                for t in range(NT):
                    sc.dma_start(out=out_d[t * 128:(t + 1) * 128, :],
                                 in_=tfa[:]).then_inc(OUT.h, 16)
                    OUT.inc(16)
            @block.gpsimd
            def _(g):
                g.wait_ge(OUT.h, OUT.n)


    return nc


_NC_CACHE = None


def kernel(**inputs):
    global _NC_CACHE
    host = _prep_host(inputs)
    x = np.ascontiguousarray(np.asarray(inputs["L0_attn_x"], np.float32))
    if _NC_CACHE is None:
        _NC_CACHE = build_program()
    nc = _NC_CACHE
    in_maps = []
    for c in range(NCORE):
        m = {"x": x[c * BC:(c + 1) * BC].reshape(T, D)}
        m.update(host)
        in_maps.append(m)
    res = run_bass_kernel_spmd(nc, in_maps, core_ids=list(range(NCORE)))
    outs = [np.asarray(res.results[c]["out"]).reshape(BC, S, D) for c in range(NCORE)]
    return np.concatenate(outs, axis=0)


# version 06
# speedup vs baseline: 1.0497x; 1.0497x over previous
import contextlib
import numpy as np
import concourse.bass as bass
import concourse.mybir as mybir
from concourse.bass_utils import run_bass_kernel_spmd
from concourse.bass_core_rust import IndirectOffsetOnAxis

F32 = mybir.dt.float32
F32R = mybir.dt.float32r
BF16 = mybir.dt.bfloat16
I32 = mybir.dt.int32
AF = mybir.ActivationFunctionType
ALU = mybir.AluOpType
AXX = mybir.AxisListType.X

B, S, D = 16, 2048, 1024
H, DK = 16, 64
NCORE = 8
BC = B // NCORE
T = BC * S
NT = T // 128
NTB = S // 128

NN, NL = 255, 256
GRP, NG, CAP = 7, 37, 320
SLOTS = NG * CAP
TRASH = SLOTS
EXT = 1152
CHG = 4
NCH = (NG + CHG - 1) // CHG
NEG = -30000.0


def _dfs_order():
    out = []
    def rec(i):
        if i >= NN:
            return
        out.append(i)
        rec(2 * i + 1)
        rec(2 * i + 2)
    rec(0)
    return np.array(out)

DFS = _dfs_order()


def _prep_host(ins):
    a = {}
    def nwT(name):
        w = np.asarray(ins[f"L0_ffn_{name}_nw"], np.float32)[DFS]
        return np.ascontiguousarray(w.T.reshape(8, 128, NN).transpose(1, 0, 2))
    pad = np.zeros((128, 8, 1), np.float32)
    a["nwT_qkv"] = np.ascontiguousarray(np.concatenate(
        [nwT("q"), pad, nwT("k"), pad, nwT("v"), pad], axis=2))
    a["nwT_f"] = np.ascontiguousarray(np.concatenate([nwT("f"), pad], axis=2))

    def nbrow(name):
        v = np.asarray(ins[f"L0_ffn_{name}_nb"], np.float32)[DFS]
        return np.concatenate([v, np.zeros(1, np.float32)])
    a["nb_qkv"] = np.ascontiguousarray(
        np.concatenate([nbrow("q"), nbrow("k"), nbrow("v")])[None, :])
    a["nb_f"] = np.ascontiguousarray(nbrow("f")[None, :])

    leaf_of = np.zeros((NG, GRP), np.int64)
    for g in range(NG):
        for b_ in range(GRP):
            lf = b_ * NG + g
            leaf_of[g, b_] = lf if lf < NL else 0
    for name in ("q", "k", "v", "f"):
        w1 = np.asarray(ins[f"L0_ffn_{name}_w1"], np.float32)
        b1 = np.asarray(ins[f"L0_ffn_{name}_b1"], np.float32)
        w2 = np.asarray(ins[f"L0_ffn_{name}_w2"], np.float32)
        b2 = np.asarray(ins[f"L0_ffn_{name}_b2"], np.float32)
        W1 = w1[leaf_of].reshape(NG, GRP, 8, 128, 16)
        a[f"W1_{name}"] = np.ascontiguousarray(
            W1.transpose(3, 0, 2, 1, 4).reshape(128, NG, 8, GRP * 16))
        a[f"b1_{name}"] = np.ascontiguousarray(b1[leaf_of].reshape(1, NG * GRP * 16))
        W2 = np.zeros((128, NG, D), np.float32)
        W2[:112] = w2[leaf_of].reshape(NG, GRP * 16, D).transpose(1, 0, 2)
        W2[112:119] = b2[leaf_of].transpose(1, 0, 2)
        a[f"W2_{name}"] = np.ascontiguousarray(W2)

    a["WdqT"] = np.ascontiguousarray(np.asarray(ins["L0_attn_W_dq"], np.float32)
                                     .reshape(8, 128, 2, 128).transpose(1, 0, 2, 3))
    a["WdkvT"] = np.ascontiguousarray(np.asarray(ins["L0_attn_W_dkv"], np.float32)
                                      .reshape(8, 128, 2, 128).transpose(1, 0, 2, 3))
    a["WuqT"] = np.ascontiguousarray(np.asarray(ins["L0_attn_W_uq"], np.float32)
                                     .reshape(2, 128, 8, 128).transpose(1, 0, 2, 3))
    a["WukT"] = np.ascontiguousarray(np.asarray(ins["L0_attn_W_uk"], np.float32)
                                     .reshape(2, 128, 8, 128).transpose(1, 0, 2, 3))
    a["Wuv"] = np.ascontiguousarray(np.asarray(ins["L0_attn_W_uv"], np.float32)
                                    .reshape(2, 128, D).transpose(1, 0, 2))
    a["Wo"] = np.ascontiguousarray(np.asarray(ins["L0_attn_W_o"], np.float32)
                                   .reshape(8, 128, D).transpose(1, 0, 2))
    return a


class Ctr:
    def __init__(self, h):
        self.h = h
        self.n = 0
    def inc(self, k=1):
        self.n += k
        return self.n


class _Cut(Exception):
    pass


SEM_TRACE = {}
import os as _os0
_DBG_DEST = _os0.environ.get("K_DBG_DEST") == "1"
_DBG_AV = _os0.environ.get("K_DBG_AV") == "1"
_DBG_LN = _os0.environ.get("K_DBG_LN") == "1"


def build_program(upto=None):
    nc = bass.Bass("TRN2", target_bir_lowering=False, detect_race_conditions=False)

    x_in = nc.dram_tensor("x", [T, D], F32, kind="ExternalInput").ap()
    out_d = nc.dram_tensor("out", [T, D], F32, kind="ExternalOutput").ap()
    hi = {}
    for nm, shape in [("nwT_qkv", [128, 8, 768]), ("nwT_f", [128, 8, 256]),
                      ("nb_qkv", [1, 768]), ("nb_f", [1, 256]),
                      ("WdqT", [128, 8, 2, 128]), ("WdkvT", [128, 8, 2, 128]),
                      ("WuqT", [128, 2, 8, 128]), ("WukT", [128, 2, 8, 128]),
                      ("Wuv", [128, 2, D]), ("Wo", [128, 8, D])]:
        hi[nm] = nc.dram_tensor(nm, shape, F32, kind="ExternalInput").ap()
    for n in ("q", "k", "v", "f"):
        hi[f"W1_{n}"] = nc.dram_tensor(f"W1_{n}", [128, NG, 8, GRP * 16], F32,
                                       kind="ExternalInput").ap()
        hi[f"b1_{n}"] = nc.dram_tensor(f"b1_{n}", [1, NG * GRP * 16], F32,
                                       kind="ExternalInput").ap()
        hi[f"W2_{n}"] = nc.dram_tensor(f"W2_{n}", [128, NG, D], F32,
                                       kind="ExternalInput").ap()

    d_x16 = nc.dram_tensor("d_x16", [T, D], BF16, kind="Internal").ap()
    d_x1n = nc.dram_tensor("d_x1n", [T, D], F32, kind="Internal").ap()
    d_xn32 = nc.dram_tensor("d_xn32", [T, D], F32, kind="Internal").ap()
    d_xn16 = nc.dram_tensor("d_xn16", [T, D], BF16, kind="Internal").ap()
    d_y32 = nc.dram_tensor("d_y32", [T, D], F32, kind="Internal").ap()
    d_y16 = nc.dram_tensor("d_y16", [T, D], BF16, kind="Internal").ap()
    d_qcan = nc.dram_tensor("d_qcan", [T, D], BF16, kind="Internal").ap()
    d_kcan = nc.dram_tensor("d_kcan", [T, D], BF16, kind="Internal").ap()
    d_vcan = nc.dram_tensor("d_vcan", [T, D], BF16, kind="Internal").ap()
    d_sortx = nc.dram_tensor("d_sortx", [12032, EXT], BF16, kind="Internal").ap()
    d_sorto = nc.dram_tensor("d_sorto", [SLOTS + 128, D], BF16, kind="Internal").ap()

    root = contextlib.ExitStack()
    with root:
        LD = Ctr(root.enter_context(nc.semaphore("LD")))
        LS = Ctr(root.enter_context(nc.semaphore("LS")))
        LG = Ctr(root.enter_context(nc.semaphore("LG")))
        TS = Ctr(root.enter_context(nc.semaphore("TSE")))
        VS = Ctr(root.enter_context(nc.semaphore("VSE")))
        SS = Ctr(root.enter_context(nc.semaphore("SSE")))
        GS = Ctr(root.enter_context(nc.semaphore("GSE")))
        OUT = Ctr(root.enter_context(nc.semaphore("OUTS")))

        def sbuf(st, name, shape, dt=BF16):
            return st.enter_context(nc.sbuf_tensor(name, list(shape), dt))

        def psum(st, name, shape):
            return st.enter_context(nc.psum_tensor(name, list(shape), F32))

        iden = sbuf(root, "iden", (128, 128), F32)
        iden16 = sbuf(root, "iden16", (128, 128), BF16)
        maskdiag = sbuf(root, "maskdiag", (128, 128), BF16)
        tristrict = sbuf(root, "tristrict", (128, 128), BF16)
        upperincl = sbuf(root, "upperincl", (128, 128), BF16)
        allones = sbuf(root, "allones", (128, 128), BF16)
        iota37 = sbuf(root, "iota37", (128, NG), F32)
        bandidx = sbuf(root, "bandidx", (128, 1), F32)
        ones_r = sbuf(root, "ones_r", (1, 128), F32)
        ones_rr = sbuf(root, "ones_rr", (1, 128), F32R)
        ones_320 = sbuf(root, "ones_320", (1, CAP), BF16)
        ones_col = sbuf(root, "ones_col", (128, 1), BF16)
        ist = sbuf(root, "ist", (128, 8), I32)
        dest_i = {n: sbuf(root, f"dest_{n}", (128, NT), I32) for n in "qkvf"}
        leafb = {n: sbuf(root, f"leaf_{n}", (128, NT), F32) for n in "qkvf"}
        bnd = {n: sbuf(root, f"bnd_{n}", (128, NT), F32) for n in "qkvf"}
        pw = sbuf(root, "pw", (128, 24, 8), F32)
        xfa = sbuf(root, "xfa", (128, D), F32)
        xfb = sbuf(root, "xfb", (128, D), F32)
        tfa = sbuf(root, "tfa", (128, D), F32)
        tfb = sbuf(root, "tfb", (128, D), F32)
        hfa = sbuf(root, "hfa", (128, D), BF16)
        hfb = sbuf(root, "hfb", (128, D), BF16)
        bn6 = sbuf(root, "bn6", (128, 12), F32)
        bn2 = sbuf(root, "bn2", (128, 2), F32)
        st1 = sbuf(root, "st1", (128, 1), F32)
        st2 = sbuf(root, "st2", (128, 1), F32)
        st3 = sbuf(root, "st3", (128, 1), F32)
        c_eps = sbuf(root, "c_eps", (128, 1), F32)
        pidx = sbuf(root, "pidx", (128, 1), F32)
        cmpv = sbuf(root, "cmpv", (128, 1), F32)
        fence = sbuf(root, "fence", (128, 1), F32)

        block = root.enter_context(nc.Block())
        ev = {}

        def barrier(tag):
            if tag.startswith("dbg_") and tag != upto:
                return
            _do_barrier(tag)
            if upto is not None and tag == upto:
                raise _Cut()

        def _do_barrier(tag):
            SEM_TRACE[tag] = dict(TS=TS.n, VS=VS.n, SS=SS.n, GS=GS.n,
                                  LD=LD.n, LS=LS.n, LG=LG.n, OUT=OUT.n)
            vals = [(TS, TS.n), (VS, VS.n), (SS, SS.n), (GS, GS.n),
                    (LD, LD.n), (LS, LS.n), (LG, LG.n)]
            def waits(e):
                for c, n in vals:
                    if n:
                        e.wait_ge(c.h, n)
            @block.gpsimd
            def _(g):
                waits(g)
            @block.tensor
            def _(tn):
                waits(tn)
            @block.vector
            def _(v):
                waits(v)
            @block.scalar
            def _(s):
                waits(s)
            @block.sync
            def _(sy):
                waits(sy)

        @block.gpsimd
        def _(g):
            g.iota(ist[:, 0:1], pattern=[[0, 1]], base=0, channel_multiplier=1)
            g.iota(ist[:, 1:1 + 1], pattern=[[0, 1]], base=0, channel_multiplier=0).then_inc(GS.h, 1)
        GS.inc()
        i37 = sbuf(root, "i37", (128, NG), I32)
        @block.gpsimd
        def _(g):
            g.iota(i37[:], pattern=[[1, NG]], base=0, channel_multiplier=0).then_inc(GS.h, 1)
        GS.inc()

        def vw(v, mm):
            mm.then_inc(VS.h, 1)
            v.wait_ge(VS.h, VS.inc())

        @block.vector
        def _(v):
            v.wait_ge(GS.h, GS.n)
            vw(v, v.tensor_copy(st1[:], ist[:, 0:1]))
            v.tensor_copy(pidx[:], ist[:, 0:1])
            v.tensor_copy(iota37[:], i37[:])
            vw(v, v.tensor_scalar_mul(st2[:], st1[:], 1.0 / 16.0))
            vw(v, v.tensor_copy(ist[:, 2:3], st2[:]))
            vw(v, v.tensor_copy(bandidx[:], ist[:, 2:3]))
            vw(v, v.tensor_tensor(st2[:], st2[:], bandidx[:], ALU.is_lt))
            vw(v, v.tensor_sub(bandidx[:], bandidx[:], st2[:]))
            vw(v, v.tensor_scalar_add(st3[:], pidx[:], -112.0))
            vw(v, v.tensor_single_scalar(st2[:], bandidx[:], 7.0, ALU.is_equal))
            vw(v, v.tensor_sub(st3[:], st3[:], bandidx[:]))
            vw(v, v.scalar_tensor_tensor(cmpv[:], st3[:], st2[:], bandidx[:],
                                         ALU.mult, ALU.add))
            for _k in range(8):
                v.memset(pw[:, :, _k:_k + 1], float(2 ** (7 - _k)))
            v.memset(ones_r[:], 1.0)
            v.memset(ones_rr[:].bitcast(F32), 1.0)
            for _n in "qkvf":
                v.memset(dest_i[_n][:], TRASH)
            v.memset(ones_320[:], 1.0)
            v.memset(ones_col[:], 1.0)
            v.memset(c_eps[:], 1e-5)
            v.memset(tfa[:, 0:128], 1.0)
            v.memset(hfa[:, 0:128], NEG)
            v.memset(hfb[:, 0:128], 1.0).then_inc(VS.h, 1)
        VS.inc()

        @block.gpsimd
        def _(g):
            g.wait_ge(VS.h, VS.n)
            g.affine_select(tfb[:, 0:128], tfa[:, 0:128], pattern=[[-1, 128]],
                            compare_op=ALU.is_ge, fill=0.0, base=0, channel_multiplier=1)
            g.affine_select(iden[:], tfb[:, 0:128], pattern=[[1, 128]],
                            compare_op=ALU.is_ge, fill=0.0, base=0, channel_multiplier=-1)
            g.affine_select(maskdiag[:], hfa[:, 0:128], pattern=[[-1, 128]],
                            compare_op=ALU.is_ge, fill=0.0, base=-1, channel_multiplier=1)
            g.affine_select(tristrict[:], hfb[:, 0:128], pattern=[[1, 128]],
                            compare_op=ALU.is_ge, fill=0.0, base=-1, channel_multiplier=-1)
            g.affine_select(upperincl[:], hfb[:, 0:128], pattern=[[-1, 128]],
                            compare_op=ALU.is_ge, fill=0.0, base=0, channel_multiplier=1)
            g.memset(allones[:], 1.0)
            g.tensor_copy(iden16[:], iden[:]).then_inc(GS.h, 1)
        GS.inc()
        _do_barrier("p0")

        _SKIP_ZF = False
        @block.vector
        def _(v):
            v.memset(hfa[:], 0.0)
            v.memset(hfb[:, 0:128], 0.0).then_inc(VS.h, 1)
        ev_zf = VS.inc()
        @block.sync
        def _(sy):
            sy.wait_ge(VS.h, ev_zf)
            for i in range(0 if _SKIP_ZF else 94):
                sy.dma_start(out=d_sortx[i * 128:(i + 1) * 128, 0:1024],
                             in_=hfa[:]).then_inc(LD.h, 16)
                sy.dma_start(out=d_sortx[i * 128:(i + 1) * 128, 1024:1152],
                             in_=hfb[:, 0:128]).then_inc(LD.h, 16)
                LD.inc(32)
            if not _SKIP_ZF:
                sy.dma_start(out=d_sorto[SLOTS:SLOTS + 128, :],
                             in_=hfa[:].bitcast(BF16) if False else hfa[:]).then_inc(LD.h, 16)
                LD.inc(16)
        try:
            barrier("zf")
        except _Cut:
            pass
        _zf_cut = (upto == "zf")


        def mla_batch(b):
            st = contextlib.ExitStack()
            _cut = [False]
            with st:
              try:
                cqT = sbuf(st, f"cqT{b}", (128, 2, S))
                ckvT = sbuf(st, f"ckvT{b}", (128, 2, S))
                QT = sbuf(st, f"QT{b}", (128, 8, S))
                KT = sbuf(st, f"KT{b}", (128, 8, S))
                Vp = sbuf(st, f"Vp{b}", (128, NTB, H * 65))
                p_a = psum(st, f"p_a{b}", (128, 2048))
                stA = st.enter_context(contextlib.ExitStack())
                xT = sbuf(stA, f"xT{b}", (128, 8, S))
                wdq = sbuf(stA, f"wdq{b}", (128, 8, 2, 128))
                wdkv = sbuf(stA, f"wdkv{b}", (128, 8, 2, 128))
                wuq = sbuf(stA, f"wuq{b}", (128, 2, 8, 128))
                wuk = sbuf(stA, f"wuk{b}", (128, 2, 8, 128))
                wuv = sbuf(stA, f"wuv{b}", (128, 2, D))

                @block.gpsimd
                def _(g):
                    for nm, dst in [("WdqT", wdq), ("WdkvT", wdkv), ("WuqT", wuq),
                                    ("WukT", wuk), ("Wuv", wuv)]:
                        g.dma_start(out=dst[:], in_=hi[nm]).then_inc(LG.h, 16)
                        LG.inc(16)
                ev["mlaw", b] = LG.n

                for t in range(NTB):
                    xb = xfa if t % 2 == 0 else xfb
                    hb = hfa if t % 2 == 0 else hfb
                    @block.sync
                    def _(sy, t=t, xb=xb):
                        if t >= 2:
                            sy.wait_ge(VS.h, ev["x16v", b, t - 2])
                        sy.dma_start(out=xb[:], in_=x_in[b * S + t * 128:
                                                         b * S + (t + 1) * 128, :]
                                     ).then_inc(LD.h, 16)
                    ev["xld", b, t] = LD.inc(16)
                    @block.vector
                    def _(v, t=t, xb=xb, hb=hb):
                        v.wait_ge(LD.h, ev["xld", b, t])
                        if t >= 2:
                            v.wait_ge(LS.h, ev["x16s", b, t - 2])
                        v.tensor_copy(hb[:], xb[:]).then_inc(VS.h, 1)
                    ev["x16v", b, t] = VS.inc()
                    @block.scalar
                    def _(s, t=t, hb=hb):
                        s.wait_ge(VS.h, ev["x16v", b, t])
                        s.dma_start(out=d_x16[b * S + t * 128: b * S + (t + 1) * 128, :],
                                    in_=hb[:]).then_inc(LS.h, 16)
                    ev["x16s", b, t] = LS.inc(16)

                barrier(f"dbg_x16_{b}")
                @block.scalar
                def _(s):
                    s.wait_ge(LS.h, ev["x16s", b, NTB - 1])
                    for c in range(8):
                        s.dma_start_transpose(
                            xT[:, c, :],
                            d_x16[b * S:(b + 1) * S, c * 128:(c + 1) * 128]
                        ).then_inc(LS.h, 16)
                        LS.inc(16)
                ev["xT", b] = LS.n

                barrier(f"dbg_xT_{b}")
                for wi, (wsrc, cdst) in enumerate(((wdq, cqT), (wdkv, ckvT))):
                    for cc in range(2):
                        @block.tensor
                        def _(tn, wsrc=wsrc, cc=cc, wi=wi):
                            if wi == 0 and cc == 0:
                                tn.wait_ge(LS.h, ev["xT", b])
                                tn.wait_ge(LG.h, ev["mlaw", b])
                            else:
                                tn.wait_ge(VS.h, ev["cev", b, wi, cc - 1] if cc > 0
                                           else ev["cev", b, wi - 1, 1])
                            _cc = cc
                            for dc in range(8):
                                for qc in range(4):
                                    mm = tn.matmul(p_a[:, qc * 512:(qc + 1) * 512],
                                                   wsrc[:, dc, _cc, :],
                                                   xT[:, dc, qc * 512:(qc + 1) * 512],
                                                   start=(dc == 0), stop=(dc == 7))
                            mm.then_inc(TS.h, 1)
                        ev["cmm", b, wi, cc] = TS.inc()
                        @block.vector
                        def _(v, cdst=cdst, cc=cc, wi=wi):
                            v.wait_ge(TS.h, ev["cmm", b, wi, cc])
                            v.tensor_copy(cdst[:, cc, :], p_a[:]).then_inc(VS.h, 1)
                        ev["cev", b, wi, cc] = VS.inc()
                        barrier(f"dbg_cp_{wi}_{cc}_{b}")

                barrier(f"dbg_cproj_{b}")
                for wi, (wsrc, csrc, qdst) in enumerate(((wuq, cqT, QT), (wuk, ckvT, KT))):
                    for hc in range(8):
                        @block.tensor
                        def _(tn, wsrc=wsrc, csrc=csrc, hc=hc, wi=wi):
                            if wi == 0 and hc == 0:
                                tn.wait_ge(VS.h, ev["cev", b, 1, 1])
                            else:
                                tn.wait_ge(VS.h, ev["qev", b, wi, hc - 1] if hc > 0
                                           else ev["qev", b, wi - 1, 7])
                            for cc in range(2):
                                for qc in range(4):
                                    mm = tn.matmul(p_a[:, qc * 512:(qc + 1) * 512],
                                                   wsrc[:, cc, hc, :],
                                                   csrc[:, cc, qc * 512:(qc + 1) * 512],
                                                   start=(cc == 0), stop=(cc == 1))
                            mm.then_inc(TS.h, 1)
                        ev["qmm", b, wi, hc] = TS.inc()
                        @block.vector
                        def _(v, qdst=qdst, hc=hc, wi=wi):
                            v.wait_ge(TS.h, ev["qmm", b, wi, hc])
                            v.tensor_copy(qdst[:, hc, :], p_a[:]).then_inc(VS.h, 1)
                        ev["qev", b, wi, hc] = VS.inc()

                barrier(f"dbg_qk_{b}")
                @block.vector
                def _(v):
                    v.memset(Vp[:].rearrange("p t (h c) -> p (t h) c", c=65)
                             [:, :, 64:65], 1.0).then_inc(VS.h, 1)
                ev["vones", b] = VS.inc()
                for t in range(NTB):
                    @block.tensor
                    def _(tn, t=t):
                        if t == 0:
                            tn.wait_ge(VS.h, ev["qev", b, 1, 7])
                        else:
                            tn.wait_ge(VS.h, ev["vev", b, t - 1])
                        for cc in range(2):
                            for half in range(2):
                                mm = tn.matmul(p_a[:, half * 512:(half + 1) * 512],
                                               ckvT[:, cc, t * 128:(t + 1) * 128],
                                               wuv[:, cc, half * 512:(half + 1) * 512],
                                               start=(cc == 0), stop=(cc == 1))
                        mm.then_inc(TS.h, 1)
                    ev["vmm", b, t] = TS.inc()
                    @block.vector
                    def _(v, t=t):
                        v.wait_ge(TS.h, ev["vmm", b, t])
                        v.wait_ge(VS.h, ev["vones", b])
                        v.tensor_copy(
                            Vp[:, t, :].rearrange("p (h c) -> p h c", c=65)[:, :, 0:64],
                            p_a[:, 0:1024].rearrange("p (h c) -> p h c", c=64)).then_inc(VS.h, 1)
                    ev["vev", b, t] = VS.inc()

                barrier(f"dbg_proj_{b}")
                if _DBG_DEST and b == 0:
                    o_cq = nc.dram_tensor("o_cq", [128, S], BF16,
                                          kind="ExternalOutput").ap()
                    o_qt = nc.dram_tensor("o_qt", [64, S], BF16,
                                          kind="ExternalOutput").ap()
                    o_kt = nc.dram_tensor("o_kt", [64, S], BF16,
                                          kind="ExternalOutput").ap()
                    o_vp = nc.dram_tensor("o_vp", [128, 16 * 65], BF16,
                                          kind="ExternalOutput").ap()
                    @block.scalar
                    def _(s):
                        s.wait_ge(VS.h, ev["vev", b, NTB - 1])
                        s.dma_start(out=o_cq, in_=cqT[:, 0, :]).then_inc(LS.h, 16)
                        s.dma_start(out=o_qt, in_=QT[0:64, 0, :]).then_inc(LS.h, 16)
                        s.dma_start(out=o_kt, in_=KT[0:64, 0, :]).then_inc(LS.h, 16)
                        s.dma_start(out=o_vp, in_=Vp[:, 0, :]).then_inc(LS.h, 16)
                        LS.inc(64)
                    ev["mdump", b] = LS.n
                    barrier("dbg_mladump")
                stA.close()
                st2_ = contextlib.ExitStack()
                with st2_:
                  try:
                    p_av = psum(st2_, f"p_av{b}", (128, 2048))
                    PTt = sbuf(st2_, f"PTt{b}", (128, S))
                    attnT = sbuf(st2_, f"attnT{b}", (128, 8, S))
                    zrow = sbuf(st2_, f"zrow{b}", (1, S), F32)
                    Bt = sbuf(st2_, f"Bt{b}", (64, S))

                    for h in range(H):
                        hc, ho = h // 2, (h % 2) * 64
                        sb_exp = SS.n

                        @block.tensor
                        def _(tn, h=h, hc=hc, ho=ho, sb_exp=sb_exp):
                            if h == 0:
                                tn.wait_ge(VS.h, ev["vev", b, NTB - 1])
                            else:
                                tn.wait_ge(VS.h, ev["attn_ev", b, h - 1])
                            for kt in range(NTB):
                                qc0 = kt // 4
                                for qc in range(qc0, 4):
                                    tn.matmul(p_a[:, qc * 512:(qc + 1) * 512],
                                              KT[ho:ho + 64, hc, kt * 128:(kt + 1) * 128],
                                              QT[ho:ho + 64, hc, qc * 512:(qc + 1) * 512],
                                              start=True, stop=(qc != qc0))
                                tn.matmul(p_a[:, kt * 128:(kt + 1) * 128],
                                          iden16[:], maskdiag[:],
                                          start=False, stop=True).then_inc(TS.h, 1)
                                ev["smm", b, h, kt] = TS.inc()
                                tn.wait_ge(SS.h, sb_exp + kt + 1)
                                for qc in range(qc0, 4):
                                    lo = max(qc * 512, kt * 128)
                                    mm = tn.matmul(p_av[0:65, lo:(qc + 1) * 512],
                                                   Vp[:, kt, h * 65:(h + 1) * 65],
                                                   PTt[:, lo:(qc + 1) * 512],
                                                   start=(kt == 0),
                                                   stop=(kt == qc * 4 + 3))
                                if kt == NTB - 1:
                                    mm.then_inc(TS.h, 1)
                                    ev["avdone", b, h] = TS.inc()

                        @block.scalar
                        def _(s, h=h, sb_exp=sb_exp):
                            for kt in range(NTB):
                                s.wait_ge(TS.h, ev["smm", b, h, kt])
                                s.activation(PTt[:, kt * 128:], p_a[:, kt * 128:],
                                             AF.Exp, scale=0.125).then_inc(SS.h, 1)
                                SS.inc()

                        @block.vector
                        def _(v, h=h):
                            v.wait_ge(TS.h, ev["avdone", b, h])
                            v.reciprocal(zrow[:], p_av[64:65, :]).then_inc(VS.h, 1)
                        ev["zr", b, h] = VS.inc()

                        @block.tensor
                        def _(tn, h=h):
                            tn.wait_ge(VS.h, ev["zr", b, h])
                            for qc in range(4):
                                mm = tn.matmul(p_a[0:64, qc * 512:(qc + 1) * 512],
                                               ones_r[:, 0:64],
                                               zrow[:, qc * 512:(qc + 1) * 512],
                                               start=True, stop=True)
                            mm.then_inc(TS.h, 1)
                        ev["bc", b, h] = TS.inc()

                        @block.vector
                        def _(v, h=h, hc=hc, ho=ho):
                            v.wait_ge(TS.h, ev["bc", b, h])
                            v.tensor_copy(Bt[:], p_a[0:64, :])
                            v.tensor_tensor(attnT[ho:ho + 64, hc, :],
                                            p_av[0:64, :], Bt[:], ALU.mult).then_inc(VS.h, 1)
                        ev["attn_ev", b, h] = VS.inc()

                        if _DBG_AV and b == 0 and h == 0:
                            o_pt = nc.dram_tensor("o_pt", [128, S], BF16,
                                                  kind="ExternalOutput").ap()
                            o_av = nc.dram_tensor("o_av", [65, S], F32,
                                                  kind="ExternalOutput").ap()
                            o_zr = nc.dram_tensor("o_zr", [1, S], F32,
                                                  kind="ExternalOutput").ap()
                            avdump = sbuf(st2_, "avdump", (65, S), F32)
                            @block.vector
                            def _(v):
                                v.wait_ge(TS.h, ev["avdone", b, 0])
                                v.tensor_copy(avdump[:], p_av[0:65, :]).then_inc(VS.h, 1)
                            ev["avcp"] = VS.inc()
                            @block.scalar
                            def _(s):
                                s.wait_ge(VS.h, ev["avcp"])
                                s.wait_ge(VS.h, ev["attn_ev", b, 0])
                                s.dma_start(out=o_pt, in_=PTt[:]).then_inc(LS.h, 16)
                                s.dma_start(out=o_av, in_=avdump[:]).then_inc(LS.h, 16)
                                s.dma_start(out=o_zr, in_=zrow[:].bitcast(F32)
                                            ).then_inc(LS.h, 16)
                                LS.inc(48)
                            barrier("dbg_avdump")

                    if _DBG_AV and b == 0:
                        o_at = nc.dram_tensor("o_at", [128, S], BF16,
                                              kind="ExternalOutput").ap()
                        @block.scalar
                        def _(s):
                            s.wait_ge(VS.h, ev["attn_ev", b, H - 1])
                            s.dma_start(out=o_at, in_=attnT[:, 0, :]).then_inc(LS.h, 16)
                        ev["atdump"] = LS.inc(16)
                        barrier("dbg_atdump")

                    barrier(f"dbg_attn_{b}")
                    st3_ = contextlib.ExitStack()
                    with st3_:
                      try:
                          wo = sbuf(st3_, f"wo{b}", (128, 8, D))
                          @block.gpsimd
                          def _(g):
                              g.wait_ge(TS.h, ev["vmm", b, NTB - 1])
                              g.dma_start(out=wo[:], in_=hi["Wo"]).then_inc(LG.h, 16)
                          ev["wol", b] = LG.inc(16)

                          for t in range(NTB):
                              xb = xfa if t % 2 == 0 else xfb
                              @block.sync
                              def _(sy, t=t, xb=xb):
                                  if t >= 2:
                                      sy.wait_ge(VS.h, ev["x1", b, t - 2])
                                  sy.dma_start(out=xb[:],
                                               in_=x_in[b * S + t * 128:
                                                        b * S + (t + 1) * 128, :]
                                               ).then_inc(LD.h, 16)
                              ev["xld2", b, t] = LD.inc(16)

                              @block.tensor
                              def _(tn, t=t):
                                  if t == 0:
                                      tn.wait_ge(VS.h, ev["attn_ev", b, H - 1])
                                      tn.wait_ge(LG.h, ev["wol", b])
                                  else:
                                      tn.wait_ge(VS.h, ev["x1", b, t - 1])
                                  for hc in range(8):
                                      for half in range(2):
                                          mm = tn.matmul(
                                              p_av[:, half * 512:(half + 1) * 512],
                                              attnT[:, hc, t * 128:(t + 1) * 128],
                                              wo[:, hc, half * 512:(half + 1) * 512],
                                              start=(hc == 0), stop=(hc == 7))
                                  mm.then_inc(TS.h, 1)
                              ev["wo", b, t] = TS.inc()

                              @block.vector
                              def _(v, t=t, xb=xb):
                                  v.wait_ge(TS.h, ev["wo", b, t])
                                  v.wait_ge(LD.h, ev["xld2", b, t])
                                  if t >= 1:
                                      v.wait_ge(LS.h, ev["lns", b, t - 1])
                                  v.tensor_add(tfa[:], p_av[:, 0:1024],
                                               xb[:]).then_inc(VS.h, 1)
                                  VS.inc()
                                  ev["x1", b, t] = VS.n
                                  v.bn_stats(bn6[:, 0:6], tfa[:, 0:512]).then_inc(VS.h, 1)
                                  _b1 = VS.inc()
                                  v.bn_stats(bn6[:, 6:12], tfa[:, 512:1024]).then_inc(VS.h, 1)
                                  _b2 = VS.inc()
                                  v.wait_ge(VS.h, _b2)
                                  v.bn_aggr(bn2[:], bn6[:]).then_inc(VS.h, 1)
                                  VS.inc()
                              ev["bn1", b, t] = VS.n

                              if _DBG_LN and b == 0 and t == 0:
                                  o_x1 = nc.dram_tensor("o_x1", [128, D], F32,
                                                        kind="ExternalOutput").ap()
                                  o_bn = nc.dram_tensor("o_bn", [128, 2], F32,
                                                        kind="ExternalOutput").ap()
                                  @block.scalar
                                  def _(s):
                                      s.wait_ge(VS.h, ev["bn1", b, 0])
                                      s.dma_start(out=o_x1, in_=tfa[:]).then_inc(LS.h, 16)
                                      s.dma_start(out=o_bn, in_=bn2[:]).then_inc(LS.h, 16)
                                      LS.inc(32)
                                  barrier("dbg_lna")

                              @block.scalar
                              def _(s, t=t):
                                  s.wait_ge(VS.h, ev["bn1", b, t])
                                  s.activation(st3[:], bn2[:, 1:2], AF.Sqrt,
                                               bias=c_eps[:]).then_inc(SS.h, 1)
                              ev["sq1", b, t] = SS.inc()

                              @block.vector
                              def _(v, t=t):
                                  v.wait_ge(SS.h, ev["sq1", b, t])
                                  v.reciprocal(st2[:], st3[:])
                                  v.tensor_scalar(tfb[:], tfa[:], bn2[:, 0:1], st2[:],
                                                  ALU.subtract, ALU.mult)
                                  v.bn_stats(bn6[:, 0:6], tfb[:, 0:512]).then_inc(VS.h, 1)
                                  _b1 = VS.inc()
                                  v.bn_stats(bn6[:, 6:12], tfb[:, 512:1024]).then_inc(VS.h, 1)
                                  _b2 = VS.inc()
                                  v.wait_ge(VS.h, _b2)
                                  v.bn_aggr(bn2[:], bn6[:]).then_inc(VS.h, 1)
                              ev["bn2", b, t] = VS.inc()

                              @block.scalar
                              def _(s, t=t):
                                  s.wait_ge(VS.h, ev["bn2", b, t])
                                  s.activation(st3[:], bn2[:, 1:2], AF.Sqrt,
                                               scale=float(D) / (D - 1))
                                  s.activation(st3[:], st3[:], AF.Sqrt,
                                               bias=c_eps[:]).then_inc(SS.h, 1)
                              ev["sq2", b, t] = SS.inc()

                              @block.vector
                              def _(v, t=t):
                                  v.wait_ge(SS.h, ev["sq2", b, t])
                                  vw(v, v.reciprocal(st2[:], st3[:]))
                                  vw(v, v.tensor_scalar(tfa[:], tfb[:], bn2[:, 0:1], st2[:],
                                                        ALU.subtract, ALU.mult))
                                  v.tensor_copy(hfa[:], tfa[:]).then_inc(VS.h, 1)
                              ev["xn", b, t] = VS.inc()

                              if _DBG_LN and b == 0 and t == 0:
                                  o_x1n = nc.dram_tensor("o_x1n", [128, D], F32,
                                                         kind="ExternalOutput").ap()
                                  o_xn = nc.dram_tensor("o_xn", [128, D], F32,
                                                        kind="ExternalOutput").ap()
                                  o_st = nc.dram_tensor("o_st", [128, 4], F32,
                                                        kind="ExternalOutput").ap()
                                  @block.vector
                                  def _(v):
                                      v.wait_ge(VS.h, ev["xn", b, 0])
                                      v.tensor_copy(bn6[:, 0:1], st2[:])
                                      v.tensor_copy(bn6[:, 1:2], st3[:])
                                      v.tensor_copy(bn6[:, 2:4], bn2[:]).then_inc(VS.h, 1)
                                  _lnb = VS.inc()
                                  @block.scalar
                                  def _(s):
                                      s.wait_ge(VS.h, _lnb)
                                      s.dma_start(out=o_x1n, in_=tfb[:]).then_inc(LS.h, 16)
                                      s.dma_start(out=o_xn, in_=tfa[:]).then_inc(LS.h, 16)
                                      s.dma_start(out=o_st, in_=bn6[:, 0:4]).then_inc(LS.h, 16)
                                      LS.inc(48)
                                  barrier("dbg_lnb")

                              @block.scalar
                              def _(s, t=t):
                                  s.wait_ge(VS.h, ev["xn", b, t])
                                  r0 = b * S + t * 128
                                  s.dma_start(out=d_x1n[r0:r0 + 128, :],
                                              in_=tfb[:]).then_inc(LS.h, 16)
                                  s.dma_start(out=d_xn32[r0:r0 + 128, :],
                                              in_=tfa[:]).then_inc(LS.h, 16)
                                  s.dma_start(out=d_xn16[r0:r0 + 128, :],
                                              in_=hfa[:]).then_inc(LS.h, 16)
                                  LS.inc(48)
                              ev["lns", b, t] = LS.n
                      except _Cut:
                          _cut[0] = True
                    if _cut[0]:
                        raise _Cut()
                  except _Cut:
                    _cut[0] = True
                if _cut[0]:
                    raise _Cut()
              except _Cut:
                _cut[0] = True
            if _cut[0]:
                raise _Cut()
            barrier(f"mla{b}")

        try:
            if _zf_cut:
                raise _Cut()
            mla_batch(0)
            mla_batch(1)

            def routing_phase(names, srcdram):
                nf = len(names)
                tag = names[0]
                width = 768 if nf == 3 else 256
                st = contextlib.ExitStack()
                with st:
                    nwsb = sbuf(st, f"s_nw_{tag}", (128, 8, width), F32R)
                    nbsb = sbuf(st, f"s_nb_{tag}", (1, width), F32R)
                    Lbuf = sbuf(st, f"L_{tag}", (128, 8, width), F32)
                    selA = sbuf(st, f"selA_{tag}", (128, 8 * nf, 128), F32)
                    selB = sbuf(st, f"selB_{tag}", (128, 8 * nf, 64), F32)
                    dsb = sbuf(st, f"dsb_{tag}", (128, 8 * nf, 8), F32)
                    mskf2 = sbuf(st, f"mskf_{tag}", (128, 8 * nf, 128), F32)
                    dif = sbuf(st, f"dif_{tag}", (128, 8 * nf, 128), F32)
                    lball = sbuf(st, f"lba_{tag}", (128, 8 * nf), F32)
                    xr = [sbuf(st, f"xr_{tag}{i}", (128, 128), F32R) for i in range(2)]
                    Gt = sbuf(st, f"Gt_{tag}", (128, NT, NG * nf), BF16)
                    prodall = sbuf(st, f"prod_{tag}", (128, NT, NG * nf), F32)
                    gtf = sbuf(st, f"gtf_{tag}", (128, NT * nf), F32)
                    iall = sbuf(st, f"iall_{tag}", (128, NT * nf), I32)
                    rkall = sbuf(st, f"rk_{tag}", (128, NT * nf), F32)
                    cnall = sbuf(st, f"cn_{tag}", (128, NT * nf), F32)
                    dsi = sbuf(st, f"dsi_{tag}", (128, NT * nf), I32)
                    p_t = psum(st, f"pt_{tag}", (128, 128))
                    p_l = psum(st, f"pl_{tag}", (128, width))
                    p_r = psum(st, f"pr_{tag}", (128, NG * nf))

                    @block.gpsimd
                    def _(g):
                        g.dma_start(out=nwsb[:], in_=hi["nwT_qkv" if nf == 3 else "nwT_f"]
                                    ).then_inc(LG.h, 16)
                        g.dma_start(out=nbsb[:], in_=hi["nb_qkv" if nf == 3 else "nb_f"]
                                    ).then_inc(LG.h, 16)
                        LG.inc(32)
                    ev["nwl", tag] = LG.n

                    @block.vector
                    def _(v):
                        v.wait_ge(LG.h, ev["nwl", tag])
                        v.tensor_copy(fence[:], pidx[:]).then_inc(VS.h, 1)
                    ev["nwpr", tag] = VS.inc()

                    for ch in range(4):
                        for tt in range(8):
                            t = ch * 8 + tt
                            xb = xfa if t % 2 == 0 else xfb
                            @block.sync
                            def _(sy, t=t, xb=xb):
                                if t >= 2:
                                    sy.wait_ge(TS.h, ev["rtp", tag, t - 2, 7])
                                sy.dma_start(out=xb[:],
                                             in_=srcdram[t * 128:(t + 1) * 128, :]
                                             ).then_inc(LD.h, 16)
                            ev["rx", tag, t] = LD.inc(16)

                            base_vs = VS.n
                            @block.tensor
                            def _(tn, t=t, xb=xb, width=width, base_vs=base_vs):
                                tn.wait_ge(LD.h, ev["rx", tag, t])
                                if t == 0:
                                    tn.wait_ge(VS.h, ev["nwpr", tag])
                                else:
                                    tn.wait_ge(VS.h, ev["lev", tag, t - 1])
                                for dc in range(8):
                                    tn.transpose(p_t[:], xb[:, dc * 128:(dc + 1) * 128],
                                                 iden[:]).then_inc(TS.h, 1)
                                    ev["rtp", tag, t, dc] = TS.inc()
                                    tn.wait_ge(VS.h, base_vs + dc + 1)
                                    if width == 768:
                                        tn.matmul(p_l[:, 0:512], xr[dc % 2][:],
                                                  nwsb[:, dc, 0:512],
                                                  start=(dc == 0), stop=False)
                                        tn.matmul(p_l[:, 512:768], xr[dc % 2][:],
                                                  nwsb[:, dc, 512:768],
                                                  start=(dc == 0), stop=False)
                                    else:
                                        tn.matmul(p_l[:, 0:256], xr[dc % 2][:],
                                                  nwsb[:, dc, 0:256],
                                                  start=(dc == 0), stop=False)
                                if width == 768:
                                    tn.matmul(p_l[:, 0:512], ones_rr[:], nbsb[:, 0:512],
                                              start=False, stop=True)
                                    mm = tn.matmul(p_l[:, 512:768], ones_rr[:],
                                                   nbsb[:, 512:768], start=False, stop=True)
                                else:
                                    mm = tn.matmul(p_l[:, 0:256], ones_rr[:], nbsb[:, 0:256],
                                                   start=False, stop=True)
                                mm.then_inc(TS.h, 1)
                            ev["rl", tag, t] = TS.inc()

                            @block.vector
                            def _(v, t=t, tt=tt, width=width, base_vs=base_vs):
                                for dc in range(8):
                                    v.wait_ge(TS.h, ev["rtp", tag, t, dc])
                                    with nc.allow_low_precision(reason="f32r routing"):
                                        v.tensor_copy(xr[dc % 2][:], p_t[:]).then_inc(VS.h, 1)
                                    VS.inc()
                                assert VS.n == base_vs + 8
                                v.wait_ge(TS.h, ev["rl", tag, t])
                                v.tensor_copy(Lbuf[:, tt, 0:width], p_l[:, 0:width]).then_inc(VS.h, 1)
                            ev["lev", tag, t] = VS.inc()

                        @block.vector
                        def _(v, ch=ch, nf=nf, names=names, width=width):
                            nb_ = 8 * nf
                            blkw = width // nf
                            L3 = Lbuf[:].rearrange("p t (f w) -> p (t f) w", w=blkw)
                            v.wait_ge(VS.h, ev["lev", tag, ch * 8 + 7])
                            W = 127
                            cur = None
                            for lvl in range(8):
                                src = L3[:, 0:nb_, :] if lvl == 0 else cur
                                vw(v, v.tensor_single_scalar(dsb[:, 0:nb_, lvl:lvl + 1],
                                                             src[:, :, 0:1], 0.0, ALU.is_gt))
                                if W >= 1:
                                    dstb = (selA if lvl % 2 == 0 else selB)[:, 0:nb_, 0:W]
                                    hi = src[:, :, 1 + W:1 + 2 * W]
                                    lo = src[:, :, 1:1 + W]
                                    v.tensor_copy(
                                        mskf2[:, 0:nb_, 0:W],
                                        dsb[:, 0:nb_, lvl:lvl + 1].broadcast_to((128, nb_, W)))
                                    vw(v, v.tensor_tensor(dif[:, 0:nb_, 0:W], hi, lo,
                                                          ALU.subtract))
                                    vw(v, v.tensor_tensor(dif[:, 0:nb_, 0:W],
                                                          dif[:, 0:nb_, 0:W],
                                                          mskf2[:, 0:nb_, 0:W], ALU.mult))
                                    vw(v, v.tensor_tensor(dstb, dif[:, 0:nb_, 0:W], lo,
                                                          ALU.add))
                                    cur = dstb
                                W = (W - 1) // 2
                            vw(v, v.tensor_tensor(dsb[:, 0:nb_, :], dsb[:, 0:nb_, :],
                                                  pw[:, 0:nb_, :], ALU.mult))
                            vw(v, v.tensor_reduce(lball[:, 0:nb_], dsb[:, 0:nb_, :],
                                                  AXX, ALU.add))
                            for i, n in enumerate(names):
                                vw(v, v.tensor_copy(
                                    leafb[n][:, ch * 8:(ch + 1) * 8],
                                    lball[:, 0:nb_].rearrange("p (t f) -> p t f",
                                                              f=nf)[:, :, i]))
                            v.tensor_copy(fence[:], pidx[:]).then_inc(VS.h, 1)
                        ev["half", tag, ch] = VS.inc()

                    if _DBG_DEST:
                        d_dlog = nc.dram_tensor(f"dlog_{tag}", [128, 8 * width], F32,
                                                kind="ExternalOutput").ap()
                        @block.sync
                        def _(sy):
                            sy.wait_ge(VS.h, ev["half", tag, 3])
                            sy.dma_start(out=d_dlog,
                                         in_=Lbuf[:].rearrange("p a b -> p (a b)")
                                         ).then_inc(LD.h, 16)
                        ev["dlog", tag] = LD.inc(16)
                        @block.vector
                        def _(v):
                            v.wait_ge(LD.h, ev["dlog", tag])
                            v.tensor_copy(fence[:], pidx[:]).then_inc(VS.h, 1)
                        VS.inc()

                    @block.vector
                    def _(v, names=names, nf=nf):
                        v.wait_ge(VS.h, ev["half", tag, 3])
                        i3 = iall[:].rearrange("p (t f) -> p t f", f=nf)
                        g3 = gtf[:].rearrange("p (t f) -> p t f", f=nf)
                        for i, n in enumerate(names):
                            vw(v, v.tensor_scalar(g3[:, :, i], leafb[n][:], 1.0 / NG,
                                                  0.5 / NG, ALU.mult, ALU.add))
                            vw(v, v.tensor_copy(i3[:, :, i], g3[:, :, i]))
                            vw(v, v.tensor_copy(bnd[n][:], i3[:, :, i]))
                            vw(v, v.tensor_tensor(g3[:, :, i], g3[:, :, i], bnd[n][:],
                                                  ALU.is_lt))
                            vw(v, v.tensor_tensor(bnd[n][:], bnd[n][:], g3[:, :, i],
                                                  ALU.subtract))
                            vw(v, v.scalar_tensor_tensor(g3[:, :, i], bnd[n][:],
                                                         -float(NG), leafb[n][:],
                                                         ALU.mult, ALU.add))
                        v.tensor_copy(fence[:], pidx[:]).then_inc(VS.h, 1)
                    ev["bandg", tag] = VS.inc()

                    for t in range(NT):
                        @block.vector
                        def _(v, t=t, nf=nf):
                            if t == 0:
                                v.wait_ge(VS.h, ev["bandg", tag])
                            for i in range(nf):
                                v.tensor_scalar(Gt[:, t, i * NG:(i + 1) * NG], iota37[:],
                                                gtf[:, t * nf + i:t * nf + i + 1],
                                                None, ALU.is_equal)
                            v.tensor_copy(fence[:], pidx[:]).then_inc(VS.h, 1)
                        ev["gt", tag, t] = VS.inc()

                        @block.tensor
                        def _(tn, t=t, nf=nf):
                            tn.wait_ge(VS.h, ev["gt", tag, t])
                            if t >= 1:
                                tn.wait_ge(VS.h, ev["dst", tag, t - 1])
                            mm = tn.matmul(p_r[:, 0:NG * nf], tristrict[:],
                                           Gt[:, t, 0:NG * nf], start=True, stop=(t == 0))
                            for j in range(t):
                                mm = tn.matmul(p_r[:, 0:NG * nf], allones[:],
                                               Gt[:, j, 0:NG * nf],
                                               start=False, stop=(j == t - 1))
                            mm.then_inc(TS.h, 1)
                        ev["rk1", tag, t] = TS.inc()

                        @block.vector
                        def _(v, t=t, nf=nf):
                            v.wait_ge(TS.h, ev["rk1", tag, t])
                            v.tensor_tensor(prodall[:, t, 0:NG * nf], Gt[:, t, 0:NG * nf],
                                            p_r[:, 0:NG * nf], ALU.mult).then_inc(VS.h, 1)
                        ev["dst", tag, t] = VS.inc()

                    @block.vector
                    def _(v, names=names, nf=nf):
                        v.wait_ge(VS.h, ev["dst", tag, NT - 1])
                        nw_ = NT * nf
                        vw(v, v.tensor_reduce(
                            rkall[:, 0:nw_],
                            prodall[:].rearrange("p t (f g) -> p (t f) g", g=NG),
                            AXX, ALU.add))
                        vw(v, v.tensor_single_scalar(cnall[:, 0:nw_], rkall[:, 0:nw_],
                                                     float(CAP), ALU.is_lt))
                        vw(v, v.scalar_tensor_tensor(rkall[:, 0:nw_], gtf[:, 0:nw_],
                                                     float(CAP), rkall[:, 0:nw_],
                                                     ALU.mult, ALU.add))
                        vw(v, v.tensor_scalar_add(rkall[:, 0:nw_], rkall[:, 0:nw_],
                                                  -float(TRASH)))
                        vw(v, v.tensor_tensor(rkall[:, 0:nw_], rkall[:, 0:nw_],
                                              cnall[:, 0:nw_], ALU.mult))
                        vw(v, v.tensor_scalar_add(rkall[:, 0:nw_], rkall[:, 0:nw_],
                                                  float(TRASH)))
                        vw(v, v.tensor_copy(dsi[:, 0:nw_], rkall[:, 0:nw_]))
                        d3 = dsi[:].rearrange("p (t f) -> p t f", f=nf)
                        for i, n in enumerate(names):
                            vw(v, v.tensor_copy(dest_i[n][:], d3[:, :, i]))
                        v.tensor_copy(fence[:], pidx[:]).then_inc(VS.h, 1)
                    ev["dstw", tag] = VS.inc()
                barrier(f"rt{tag}")

            routing_phase(["q", "k", "v"], d_xn32)

            if _DBG_DEST:
                d_dbgx = nc.dram_tensor("dbgx", [128, 2 * D], F32,
                                        kind="ExternalOutput").ap()
                @block.sync
                def _(sy):
                    sy.dma_start(out=xfa[:], in_=d_xn32[0:128, :]).then_inc(LD.h, 16)
                    sy.dma_start(out=xfb[:], in_=d_x1n[0:128, :]).then_inc(LD.h, 16)
                ev["dbgxl"] = LD.inc(32)
                @block.scalar
                def _(s):
                    s.wait_ge(LD.h, ev["dbgxl"])
                    s.dma_start(out=d_dbgx[:, 0:D], in_=xfa[:]).then_inc(LD.h, 16)
                    s.dma_start(out=d_dbgx[:, D:2 * D], in_=xfb[:]).then_inc(LD.h, 16)
                ev["dbgxs"] = LD.inc(32)
                @block.vector
                def _(v):
                    v.wait_ge(LD.h, ev["dbgxs"])
                    v.tensor_copy(fence[:], pidx[:]).then_inc(VS.h, 1)
                VS.inc()
                d_dbg = nc.dram_tensor("dbg", [128, 6 * NT], F32,
                                       kind="ExternalOutput").ap()
                dbf = sbuf(root, "dbf", (128, 3 * NT), F32)
                @block.vector
                def _(v):
                    for i, n in enumerate("qkv"):
                        v.tensor_copy(dbf[:, i * NT:(i + 1) * NT], dest_i[n][:])
                    v.tensor_copy(fence[:], pidx[:]).then_inc(VS.h, 1)
                ev["dbgc"] = VS.inc()
                @block.sync
                def _(sy):
                    sy.wait_ge(VS.h, ev["dbgc"])
                    sy.dma_start(out=d_dbg[:, 0:3 * NT], in_=dbf[:]).then_inc(LD.h, 16)
                    for i, n in enumerate("qkv"):
                        sy.dma_start(out=d_dbg[:, (3 + i) * NT:(4 + i) * NT],
                                     in_=leafb[n][:]).then_inc(LD.h, 16)
                    LD.inc(64)
                barrier("dbg_destdump")

            def legs_phase(name, src16, act_fn, outdram):
                st = contextlib.ExitStack()
                _cut = [False]
                with st:
                  try:
                      extt = [sbuf(st, f"ext{name}{i}", (128, EXT)) for i in range(2)]
                      xTs = [sbuf(st, f"xTs{name}{i}", (128, 9, CHG * CAP)) for i in range(2)]
                      w1c = [sbuf(st, f"w1c{name}{i}", (128, CHG, 8, GRP * 16))
                             for i in range(2)]
                      w2c = [sbuf(st, f"w2c{name}{i}", (128, CHG, D)) for i in range(2)]
                      b1c = [sbuf(st, f"b1c{name}{i}", (1, CHG * GRP * 16)) for i in range(2)]
                      hsb = sbuf(st, f"hsb{name}", (128, CAP))
                      msb = sbuf(st, f"msb{name}", (128, CAP))
                      hpk = [sbuf(st, f"hpk{name}{i}", (128, CAP)) for i in range(2)]
                      sot = [sbuf(st, f"sot{name}{i}", (128, D)) for i in range(2)]
                      cant = [sbuf(st, f"cant{name}{i}", (128, D)) for i in range(2)]
                      p_w1 = psum(st, f"pw1{name}", (112, CAP))
                      p_o = psum(st, f"po{name}", (128, D))

                      @block.vector
                      def _(v):
                          v.memset(hpk[0][:], 0.0)
                          v.memset(hpk[1][:], 0.0)
                          v.memset(hsb[96:128, :], 1.0)
                          v.tensor_copy(fence[:], pidx[:]).then_inc(VS.h, 1)
                      ev["hz", name] = VS.inc()

                      for t in range(NT):
                          hb = hfa if t % 2 == 0 else hfb
                          @block.sync
                          def _(sy, t=t, hb=hb):
                              if t >= 2:
                                  sy.wait_ge(VS.h, ev["ex", name, t - 2])
                              sy.dma_start(out=hb[:], in_=src16[t * 128:(t + 1) * 128, :]
                                           ).then_inc(LD.h, 16)
                          ev["sx", name, t] = LD.inc(16)

                          @block.vector
                          def _(v, t=t, hb=hb):
                              v.wait_ge(LD.h, ev["sx", name, t])
                              if t >= 2:
                                  v.wait_ge(LG.h, ev["sc", name, t - 2])
                              e = extt[t % 2]
                              v.tensor_copy(e[:, 0:D], hb[:])
                              vw(v, v.tensor_copy(
                                  tfb[:, 0:128],
                                  bnd[name][:, t:t + 1].broadcast_to((128, 128))))
                              v.tensor_copy(e[:, D:EXT], tfb[:, 0:128]).then_inc(VS.h, 1)
                          ev["ex", name, t] = VS.inc()

                          @block.gpsimd
                          def _(g, t=t):
                              g.wait_ge(VS.h, ev["ex", name, t])
                              g.indirect_dma_start(
                                  d_sortx, IndirectOffsetOnAxis(dest_i[name][:, t:t + 1], 0),
                                  extt[t % 2][:], None).then_inc(LG.h, 16)
                          ev["sc", name, t] = LG.inc(16)

                      barrier(f"dbg_lsc_{name}")

                      w1done, w2done, actdone, soevs, sosts = [], [], [], {}, {}
                      gidx = 0
                      for ch in range(NCH):
                          g0 = ch * CHG
                          ng = min(CHG, NG - g0)
                          bufi = ch % 2
                          @block.scalar
                          def _(s, ch=ch, g0=g0, ng=ng, bufi=bufi):
                              if ch == 0:
                                  s.wait_ge(LG.h, ev["sc", name, NT - 1])
                              if ch >= 2:
                                  s.wait_ge(TS.h, w1done[-1])
                                  gprev = (ch - 2) * CHG + min(CHG, NG - (ch - 2) * CHG) - 1
                                  s.wait_ge(VS.h, ev["hp", name, gprev])
                              for strip in range(9):
                                  s.dma_start_transpose(
                                      xTs[bufi][:, strip, 0:ng * CAP],
                                      d_sortx[g0 * CAP:g0 * CAP + ng * CAP,
                                              strip * 128:(strip + 1) * 128]
                                  ).then_inc(LS.h, 16)
                                  LS.inc(16)
                          ev["xts", name, ch] = LS.n

                          @block.gpsimd
                          def _(g, ch=ch, g0=g0, ng=ng, bufi=bufi):
                              if ch >= 2:
                                  g.wait_ge(TS.h, w2done[-1])
                              g.dma_start(out=w1c[bufi][:, 0:ng, :, :],
                                          in_=hi[f"W1_{name}"][:, g0:g0 + ng, :, :]
                                          ).then_inc(LG.h, 16)
                              g.dma_start(out=w2c[bufi][:, 0:ng, :],
                                          in_=hi[f"W2_{name}"][:, g0:g0 + ng, :]
                                          ).then_inc(LG.h, 16)
                              g.dma_start(out=b1c[bufi][:, 0:ng * GRP * 16],
                                          in_=hi[f"b1_{name}"][:, g0 * GRP * 16:
                                                               (g0 + ng) * GRP * 16]
                                          ).then_inc(LG.h, 16)
                              LG.inc(48)
                          ev["wl", name, ch] = LG.n

                          for gi in range(ng):
                              hb_ = hpk[gidx % 2]
                              @block.tensor
                              def _(tn, ch=ch, gi=gi, bufi=bufi):
                                  if gi == 0:
                                      tn.wait_ge(LS.h, ev["xts", name, ch])
                                      tn.wait_ge(LG.h, ev["wl", name, ch])
                                      if ch == 0:
                                          tn.wait_ge(VS.h, ev["hz", name])
                                  if actdone:
                                      tn.wait_ge(SS.h, actdone[-1])
                                  c0 = gi * CAP
                                  for dc in range(8):
                                      tn.matmul(p_w1[:], w1c[bufi][:, gi, dc, :],
                                                xTs[bufi][:, dc, c0:c0 + CAP],
                                                start=(dc == 0), stop=False)
                                  tn.matmul(p_w1[:], b1c[bufi][0:1, gi * 112:(gi + 1) * 112],
                                            ones_320[:], start=False,
                                            stop=True).then_inc(TS.h, 1)
                              w1done.append(TS.inc())

                              @block.scalar
                              def _(s, act_fn=act_fn):
                                  s.wait_ge(TS.h, w1done[-1])
                                  s.activation(hsb[0:112, :], p_w1[:], act_fn).then_inc(SS.h, 1)
                              actdone.append(SS.inc())

                              @block.vector
                              def _(v, ch=ch, gi=gi, bufi=bufi, hb_=hb_, gidx=gidx):
                                  c0 = gi * CAP
                                  v.wait_ge(SS.h, actdone[-1])
                                  if gidx >= 2:
                                      v.wait_ge(TS.h, w2done[gidx - 2])
                                  vw(v, v.tensor_scalar(msb[:], xTs[bufi][:, 8, c0:c0 + CAP],
                                                        cmpv[:], None, ALU.is_equal))
                                  v.tensor_tensor(hb_[:], hsb[:], msb[:], ALU.mult).then_inc(VS.h, 1)
                              ev["hp", name, gidx] = VS.inc()

                              for sbk in range(3):
                                  w = 128 if sbk < 2 else CAP - 256
                                  @block.tensor
                                  def _(tn, ch=ch, gi=gi, bufi=bufi, sbk=sbk, w=w,
                                        hb_=hb_, gidx=gidx):
                                      if sbk == 0:
                                          tn.wait_ge(VS.h, ev["hp", name, gidx])
                                      if soevs:
                                          tn.wait_ge(VS.h, soevs[max(soevs)])
                                      for half in range(2):
                                          mm = tn.matmul(
                                              p_o[0:w, half * 512:(half + 1) * 512],
                                              hb_[:, sbk * 128:sbk * 128 + w],
                                              w2c[bufi][:, gi, half * 512:(half + 1) * 512],
                                              start=True, stop=True)
                                      mm.then_inc(TS.h, 1)
                                  ev["w2", name, gidx, sbk] = TS.inc()
                                  if sbk == 2:
                                      w2done.append(ev["w2", name, gidx, 2])

                                  @block.vector
                                  def _(v, gidx=gidx, sbk=sbk, w=w):
                                      v.wait_ge(TS.h, ev["w2", name, gidx, sbk])
                                      if (gidx, sbk - 2) in sosts:
                                          v.wait_ge(LS.h, sosts[(gidx, sbk - 2)])
                                      elif sbk < 2 and (gidx - 1, sbk + 1) in sosts:
                                          v.wait_ge(LS.h, sosts[(gidx - 1, sbk + 1)])
                                      v.tensor_copy(sot[sbk % 2][0:w, :], p_o[0:w, :]).then_inc(VS.h, 1)
                                  soevs[(gidx, sbk)] = VS.inc()

                                  @block.scalar
                                  def _(s, ch=ch, gi=gi, g0=g0, gidx=gidx, sbk=sbk, w=w):
                                      s.wait_ge(VS.h, soevs[(gidx, sbk)])
                                      r0 = (g0 + gi) * CAP + sbk * 128
                                      s.dma_start(out=d_sorto[r0:r0 + w, :],
                                                  in_=sot[sbk % 2][0:w, :]).then_inc(LS.h, 16)
                                  sosts[(gidx, sbk)] = LS.inc(16)
                              gidx += 1
                          barrier(f"dbg_lch_{name}_{ch}")

                      for t in range(NT):
                          @block.gpsimd
                          def _(g, t=t):
                              if t == 0:
                                  g.wait_ge(LS.h, sosts[(NG - 1, 2)])
                              if t >= 2:
                                  g.wait_ge(LS.h, ev["cst", name, t - 2])
                              g.indirect_dma_start(
                                  cant[t % 2][:], None, d_sorto,
                                  IndirectOffsetOnAxis(dest_i[name][:, t:t + 1], 0)
                              ).then_inc(LG.h, 16)
                          ev["cg", name, t] = LG.inc(16)

                          @block.scalar
                          def _(s, t=t):
                              s.wait_ge(LG.h, ev["cg", name, t])
                              s.dma_start(out=outdram[t * 128:(t + 1) * 128, :],
                                          in_=cant[t % 2][:]).then_inc(LS.h, 16)
                          ev["cst", name, t] = LS.inc(16)
                  except _Cut:
                      _cut[0] = True
                if _cut[0]:
                    raise _Cut()
                barrier(f"legs{name}")

            legs_phase("v", d_xn16, AF.Relu, d_vcan)
            legs_phase("k", d_xn16, AF.Relu, d_kcan)
            legs_phase("q", d_xn16, AF.Relu, d_qcan)

            def ffn_attn(b):
                st = contextlib.ExitStack()
                with st:
                    qT2 = sbuf(st, f"qT2{b}", (128, 8, S))
                    kT2 = sbuf(st, f"kT2{b}", (128, 8, S))
                    PT2 = sbuf(st, f"PT2{b}", (128, NTB, S))
                    vtok = sbuf(st, f"vtok{b}", (128, NTB, D))
                    p_s = psum(st, f"ps{b}", (128, 2048))
                    p_o = psum(st, f"po2{b}", (128, D))
                    p_1 = psum(st, f"p1{b}", (128, 1))

                    @block.scalar
                    def _(s):
                        for c in range(8):
                            s.dma_start_transpose(
                                qT2[:, c, :], d_qcan[b * S:(b + 1) * S,
                                                     c * 128:(c + 1) * 128]
                            ).then_inc(LS.h, 16)
                            s.dma_start_transpose(
                                kT2[:, c, :], d_kcan[b * S:(b + 1) * S,
                                                     c * 128:(c + 1) * 128]
                            ).then_inc(LS.h, 16)
                            LS.inc(32)
                    ev["qkT", b] = LS.n

                    @block.sync
                    def _(sy):
                        sy.dma_start(
                            out=vtok[:],
                            in_=d_vcan[b * S:(b + 1) * S, :]
                            .rearrange("(t p) d -> p t d", p=128)).then_inc(LD.h, 16)
                    ev["vl", b] = LD.inc(16)

                    for kt in range(NTB):
                        @block.tensor
                        def _(tn, kt=kt):
                            if kt == 0:
                                tn.wait_ge(LS.h, ev["qkT", b])
                            else:
                                tn.wait_ge(SS.h, ev["fexp", b, kt - 1])
                            for dc in range(8):
                                for qc in range(4):
                                    mm = tn.matmul(p_s[:, qc * 512:(qc + 1) * 512],
                                                   kT2[:, dc, kt * 128:(kt + 1) * 128],
                                                   qT2[:, dc, qc * 512:(qc + 1) * 512],
                                                   start=(dc == 0), stop=(dc == 7))
                            mm.then_inc(TS.h, 1)
                        ev["fs", b, kt] = TS.inc()

                        @block.scalar
                        def _(s, kt=kt):
                            s.wait_ge(TS.h, ev["fs", b, kt])
                            s.activation(PT2[:, kt, :], p_s[:], AF.Exp,
                                         scale=1.0 / 32.0).then_inc(SS.h, 1)
                        ev["fexp", b, kt] = SS.inc()

                    for qt in range(NTB):
                        xb = xfa if qt % 2 == 0 else xfb
                        @block.sync
                        def _(sy, qt=qt, xb=xb):
                            if qt >= 2:
                                sy.wait_ge(VS.h, ev["y", b, qt - 2])
                            sy.dma_start(out=xb[:],
                                         in_=d_xn32[b * S + qt * 128:
                                                    b * S + (qt + 1) * 128, :]
                                         ).then_inc(LD.h, 16)
                        ev["xnl", b, qt] = LD.inc(16)

                        @block.tensor
                        def _(tn, qt=qt):
                            if qt == 0:
                                tn.wait_ge(SS.h, ev["fexp", b, NTB - 1])
                                tn.wait_ge(LD.h, ev["vl", b])
                            else:
                                tn.wait_ge(VS.h, ev["y", b, qt - 1])
                            for kt in range(NTB):
                                tn.matmul(p_1[:], PT2[:, kt, qt * 128:(qt + 1) * 128],
                                          ones_col[:], start=(kt == 0),
                                          stop=(kt == NTB - 1))
                                for half in range(2):
                                    mm = tn.matmul(p_o[:, half * 512:(half + 1) * 512],
                                                   PT2[:, kt, qt * 128:(qt + 1) * 128],
                                                   vtok[:, kt, half * 512:(half + 1) * 512],
                                                   start=(kt == 0), stop=(kt == NTB - 1))
                            mm.then_inc(TS.h, 1)
                        ev["fav", b, qt] = TS.inc()

                        @block.vector
                        def _(v, qt=qt, xb=xb):
                            v.wait_ge(TS.h, ev["fav", b, qt])
                            v.wait_ge(LD.h, ev["xnl", b, qt])
                            if qt >= 1:
                                v.wait_ge(LS.h, ev["ys", b, qt - 1])
                            vw(v, v.reciprocal(st1[:], p_1[:]))
                            vw(v, v.scalar_tensor_tensor(tfa[:], p_o[:], st1[:], xb[:],
                                                         ALU.mult, ALU.add))
                            v.tensor_copy(hfa[:], tfa[:]).then_inc(VS.h, 1)
                        ev["y", b, qt] = VS.inc()

                        @block.scalar
                        def _(s, qt=qt):
                            s.wait_ge(VS.h, ev["y", b, qt])
                            r0 = b * S + qt * 128
                            s.dma_start(out=d_y32[r0:r0 + 128, :],
                                        in_=tfa[:]).then_inc(LS.h, 16)
                            s.dma_start(out=d_y16[r0:r0 + 128, :],
                                        in_=hfa[:]).then_inc(LS.h, 16)
                            LS.inc(32)
                        ev["ys", b, qt] = LS.n
                barrier(f"fattn{b}")

            ffn_attn(0)
            ffn_attn(1)

            routing_phase(["f"], d_y32)
            legs_phase("f", d_y16, AF.Gelu, d_qcan)

            for t in range(NT):
                xb = xfa if t % 2 == 0 else xfb
                tb = tfa if t % 2 == 0 else tfb
                hb = hfa if t % 2 == 0 else hfb
                @block.sync
                def _(sy, t=t, xb=xb, tb=tb, hb=hb):
                    if t >= 2:
                        sy.wait_ge(VS.h, ev["z", t - 2])
                    sy.dma_start(out=xb[:], in_=d_x1n[t * 128:(t + 1) * 128, :]
                                 ).then_inc(LD.h, 16)
                    sy.dma_start(out=tb[:], in_=d_y32[t * 128:(t + 1) * 128, :]
                                 ).then_inc(LD.h, 16)
                    sy.dma_start(out=hb[:], in_=d_qcan[t * 128:(t + 1) * 128, :]
                                 ).then_inc(LD.h, 16)
                ev["fl", t] = LD.inc(48)

                @block.vector
                def _(v, t=t, xb=xb, tb=tb, hb=hb):
                    v.wait_ge(LD.h, ev["fl", t])
                    if t >= 2:
                        v.wait_ge(OUT.h, ev["os", t - 2])
                    v.tensor_add(xb[:], xb[:], tb[:])
                    v.tensor_tensor(xb[:], xb[:], hb[:], ALU.add)
                    v.bn_stats(bn6[:, 0:6], xb[:, 0:512]).then_inc(VS.h, 1)
                    _b1 = VS.inc()
                    v.bn_stats(bn6[:, 6:12], xb[:, 512:1024]).then_inc(VS.h, 1)
                    _b2 = VS.inc()
                    v.wait_ge(VS.h, _b2)
                    v.bn_aggr(bn2[:], bn6[:]).then_inc(VS.h, 1)
                ev["zbn", t] = VS.inc()

                @block.scalar
                def _(s, t=t):
                    s.wait_ge(VS.h, ev["zbn", t])
                    s.activation(st3[:], bn2[:, 1:2], AF.Sqrt, bias=c_eps[:]).then_inc(SS.h, 1)
                ev["zsq", t] = SS.inc()

                @block.vector
                def _(v, t=t, xb=xb, tb=tb):
                    v.wait_ge(SS.h, ev["zsq", t])
                    v.reciprocal(st2[:], st3[:])
                    v.tensor_scalar(tb[:], xb[:], bn2[:, 0:1], st2[:],
                                    ALU.subtract, ALU.mult).then_inc(VS.h, 1)
                ev["z", t] = VS.inc()

                @block.scalar
                def _(s, t=t, tb=tb):
                    s.wait_ge(VS.h, ev["z", t])
                    s.dma_start(out=out_d[t * 128:(t + 1) * 128, :],
                                in_=tb[:]).then_inc(OUT.h, 16)
                ev["os", t] = OUT.inc(16)

            @block.gpsimd
            def _(g):
                g.wait_ge(OUT.h, OUT.n)
        except _Cut:
            pass
        if upto is not None:
            o_cst = nc.dram_tensor("o_cst", [128, 6 * 128], F32,
                                   kind="ExternalOutput").ap()
            o_dst = nc.dram_tensor("o_dst", [128, 4 * NT], F32,
                                   kind="ExternalOutput").ap()
            @block.vector
            def _(v):
                v.memset(tfa[:], 0.0)
                v.tensor_copy(tfb[:, 0:128], iden16[:])
                v.tensor_copy(tfb[:, 128:256], maskdiag[:])
                v.tensor_copy(tfb[:, 256:384], tristrict[:])
                v.tensor_copy(tfb[:, 384:512], allones[:])
                v.tensor_copy(tfb[:, 512:640], upperincl[:])
                v.tensor_copy(tfb[:, 640:768], iden[:])
                for i, n in enumerate("qkvf"):
                    v.tensor_copy(xfa[:, i * NT:(i + 1) * NT], dest_i[n][:])
                v.tensor_copy(fence[:], pidx[:]).then_inc(VS.h, 1)
            _zout = VS.inc()
            @block.scalar
            def _(sc):
                sc.wait_ge(GS.h, GS.n)
                sc.wait_ge(VS.h, _zout)
                sc.dma_start(out=o_cst, in_=tfb[:, 0:768]).then_inc(OUT.h, 16)
                sc.dma_start(out=o_dst, in_=xfa[:, 0:4 * NT]).then_inc(OUT.h, 16)
                OUT.inc(32)
                for t in range(NT):
                    sc.dma_start(out=out_d[t * 128:(t + 1) * 128, :],
                                 in_=tfa[:]).then_inc(OUT.h, 16)
                    OUT.inc(16)
            @block.gpsimd
            def _(g):
                g.wait_ge(OUT.h, OUT.n)


    return nc


_NC_CACHE = None


def kernel(**inputs):
    global _NC_CACHE
    host = _prep_host(inputs)
    x = np.ascontiguousarray(np.asarray(inputs["L0_attn_x"], np.float32))
    if _NC_CACHE is None:
        _NC_CACHE = build_program()
    nc = _NC_CACHE
    in_maps = []
    for c in range(NCORE):
        m = {"x": x[c * BC:(c + 1) * BC].reshape(T, D)}
        m.update(host)
        in_maps.append(m)
    res = run_bass_kernel_spmd(nc, in_maps, core_ids=list(range(NCORE)))
    outs = [np.asarray(res.results[c]["out"]).reshape(BC, S, D) for c in range(NCORE)]
    return np.concatenate(outs, axis=0)
